# version 04
# baseline (speedup 1.0000x reference)
import numpy as np
import concourse.bass as bass
import concourse.mybir as mybir
from concourse.bass_utils import run_bass_kernel_spmd

F32 = mybir.dt.float32
BF16 = mybir.dt.bfloat16
U8 = mybir.dt.uint8
AF = mybir.ActivationFunctionType
ALU = mybir.AluOpType
AX = mybir.AxisListType

TOK = 4096
D = 1024
DI = 2048
DS = 16
NE = 8
T1 = 512
NT1 = TOK // T1
G1 = T1 // 128
NG = 1 + NT1 * G1
T2 = 1024
NT2 = TOK // T2
G2 = T2 // 128
CH = 256
NCH = TOK // CH
EPS = 1e-5
KB = 1024

_CACHE = {}
_DBG = False
_PHASES = "full"
_HALO_MODES = ("H", "Hm", "He", "Heb")
_NT1X = 8


class _DrainInst:

    def __init__(self, eng, ins):
        self._eng = eng
        self._ins = ins

    def then_inc(self, sem_, k=1):
        self._eng.drain().then_inc(sem_, k)
        return self


class _DrainingEngine:
    _RAW = {"wait_ge", "drain", "nop", "dma_start", "then_inc"}

    def __init__(self, eng):
        self._e = eng

    def __getattr__(self, name):
        f = getattr(self._e, name)
        if name in _DrainingEngine._RAW or not callable(f):
            return f

        def wrap(*a, **kw):
            return _DrainInst(self._e, f(*a, **kw))

        return wrap


class QSem:

    def __init__(self, mk, name, nslot):
        self.s = [mk(f"{name}{i}") for i in range(nslot)]
        self.nuse = [0] * nslot
        self.total = 0

    def issue(self, eng, mkdma):
        j = self.total
        sl = j % len(self.s)
        if self.nuse[sl] > 0:
            eng.wait_ge(self.s[sl], 16 * self.nuse[sl])
        mkdma().then_inc(self.s[sl], 16)
        self.nuse[sl] += 1
        self.total += 1
        return j

    def wait_done(self, eng, j):
        assert j >= 0
        eng.wait_ge(self.s[j % len(self.s)], 16 * (j // len(self.s) + 1))

    def wait_all(self, eng, upto):
        for sl in range(len(self.s)):
            cnt = (upto - sl + len(self.s) - 1) // len(self.s)
            if cnt > 0:
                eng.wait_ge(self.s[sl], 16 * cnt)


def _build():
    nc = bass.Bass("TRN2", target_bir_lowering=False, detect_race_conditions=False)

    dr = {}
    dr["x"] = nc.dram_tensor("x", [TOK, D], F32, kind="ExternalInput")
    dr["xh"] = nc.dram_tensor("xh", [3, D], F32, kind="ExternalInput")
    dr["hmask"] = nc.dram_tensor("hmask", [128, 1], F32, kind="ExternalInput")
    dr["cst"] = nc.dram_tensor("cst", [128, 4], F32, kind="ExternalInput")
    dr["sel"] = nc.dram_tensor("sel", [16, 3], F32, kind="ExternalInput")
    dr["selinv"] = nc.dram_tensor("selinv", [16, 3], F32, kind="ExternalInput")
    dr["wi"] = nc.dram_tensor("wi", [2064, D], F32, kind="ExternalInput")
    dr["bi"] = nc.dram_tensor("bi", [2176], F32, kind="ExternalInput")
    dr["wcdiag"] = nc.dram_tensor("wcdiag", [16, 4, 128, 128], F32, kind="ExternalInput")
    dr["bc"] = nc.dram_tensor("bc", [DI], F32, kind="ExternalInput")
    dr["wdbl"] = nc.dram_tensor("wdbl", [48, DI], F32, kind="ExternalInput")
    dr["b48"] = nc.dram_tensor("b48", [48, 1], F32, kind="ExternalInput")
    dr["wo16"] = nc.dram_tensor("wo16", [D, DS], F32, kind="ExternalInput")
    dr["gw"] = nc.dram_tensor("gw", [D, NE], F32, kind="ExternalInput")
    dr["gb"] = nc.dram_tensor("gb", [NE, 1], F32, kind="ExternalInput")
    dr["ew"] = nc.dram_tensor("ew", [NE, D, D], F32, kind="ExternalInput")
    dr["eb"] = nc.dram_tensor("eb", [NE, D], F32, kind="ExternalInput")
    dr["idin"] = nc.dram_tensor("idin", [128, 128], F32, kind="ExternalInput")
    dr["out"] = nc.dram_tensor("out", [TOK, D], F32, kind="ExternalOutput")
    stg_kind = "ExternalOutput" if _DBG else "Internal"
    dr["xn_dram"] = nc.dram_tensor("xn_dram", [8, 128, TOK], BF16, kind=stg_kind)
    dr["dbl_dram"] = nc.dram_tensor("dbl_dram", [48, TOK], F32, kind=stg_kind)
    dr["z16_dram"] = nc.dram_tensor("z16_dram", [16, TOK], F32, kind=stg_kind)
    dr["x16_dram"] = nc.dram_tensor("x16_dram", [16, TOK], F32, kind=stg_kind)
    dr["gate_dram"] = nc.dram_tensor("gate_dram", [8, TOK], F32, kind=stg_kind)
    dr["ag_in"] = nc.dram_tensor("ag_in", [16, 2], F32)
    dr["ag_out"] = nc.dram_tensor("ag_out", [4, 16, 2], F32)
    if _DBG:
        dr["d_wtm"] = nc.dram_tensor("d_wtm", [128, 256], F32, kind="ExternalOutput")
        dr["d_wfm"] = nc.dram_tensor("d_wfm", [8, TOK], BF16, kind="ExternalOutput")
        dr["d_hz"] = nc.dram_tensor("d_hz", [16, TOK], BF16, kind="ExternalOutput")
        dr["d_hloc"] = nc.dram_tensor("d_hloc", [16, TOK], F32, kind="ExternalOutput")
        dr["d_cf"] = nc.dram_tensor("d_cf", [16, 8], F32, kind="ExternalOutput")
        dr["d_gath"] = nc.dram_tensor("d_gath", [16, 8], F32, kind="ExternalOutput")
        dr["d_lnb"] = nc.dram_tensor("d_lnb", [128, 16], F32, kind="ExternalOutput")

    pend = []

    class Region:
        def __init__(self, base):
            self.off = base

        def take(self, name, shape, dtype):
            per = int(np.prod(shape[1:])) * mybir.dt.size(dtype)
            per = (per + 31) & ~31
            pend.append((name, shape, dtype, self.off))
            self.off += per

    P = Region(0)
    for nm, sh, dt in [
        ("idbf", [128, 128], BF16), ("idf32", [128, 128], F32),
        ("bi_sb", [128, 17], F32), ("bc_sb", [128, 16], F32),
        ("b48_sb", [48, 1], F32), ("gb_sb", [8, 1], F32),
        ("hm_sb", [128, 1], F32), ("cst_sb", [128, 4], F32),
        ("sel_sb", [16, 3], F32),
        ("seli_sb", [16, 3], F32), ("eb_sb", [8, D], BF16),
        ("gw_sb", [128, 8, 8], BF16), ("wo16m", [16, D], BF16),
        ("wfm", [8, TOK], BF16), ("wtm", [128, 256], F32),
        ("hz_bf", [16, TOK], BF16), ("haloxz", [128, 48], BF16),
        ("xnh_fm", [128, 8, 3], BF16), ("tmp48", [128, 48], F32),
    ]:
        P.take(nm, sh, dt)
    pbase = P.off

    A = Region(pbase)
    for nm, sh, dt in [
        ("wi_lhsT", [128, 8, 2064], BF16), ("diag_sb", [128, 16, 4, 128], BF16),
        ("wdbl_lhsT", [128, 16, 48], BF16),
        ("wstage0", [128, D], BF16), ("wstage1", [128, D], BF16),
        ("wdblst", [48, DI], BF16), ("wo16st", [128, 128], BF16),
        ("xbuf0", [128, G1 * D], F32), ("xbuf1", [128, G1 * D], F32),
        ("lnbuf", [128, 16], F32), ("sqdump", [128, D], F32),
        ("xn_tm0", [128, D], BF16), ("xn_tm1", [128, D], BF16),
        ("xnf0", [128, 8, T1], BF16), ("xnf1", [128, 8, T1], BF16),
        ("xn1b0", [128, 8, T1], BF16), ("xn1b1", [128, 8, T1], BF16),
        ("cbuf0", [128, 515], BF16), ("cbuf1", [128, 515], BF16),
        ("xcb0", [128, T1], BF16), ("xcb1", [128, T1], BF16),
        ("dstg0", [48, T1], F32), ("dstg1", [48, T1], F32),
        ("gstg0", [8, T1], F32), ("gstg1", [8, T1], F32),
        ("zstg0", [16, T1], F32), ("zstg1", [16, T1], F32),
        ("xstg0", [16, T1], F32), ("xstg1", [16, T1], F32),
    ]:
        A.take(nm, sh, dt)

    B = Region(pbase)
    blist = [
        ("xn2b0", [128, 8, T2], BF16),
        ("ebuf0", [128, 8, D], BF16), ("ebuf1", [128, 8, D], BF16),
        ("x2r0", [128, D], F32), ("x2r1", [128, D], F32), ("x2r2", [128, D], F32),
        ("acc", [128, 8, D], F32),
        ("zeros16", [16, CH], F32),
        ("h_loc", [16, TOK], F32), ("cumP", [16, TOK], F32),
        ("gfull", [8, TOK], F32),
        ("glog", [128, 8], F32), ("esb", [128, 8], F32), ("psb", [128, 8], F32),
        ("msk", [128, 8], F32), ("sft", [128, 4], F32), ("sftn", [128, 1], F32),
        ("gmx", [128, 8], F32),
        ("smry", [16, 2], F32), ("gath", [16, 8], F32), ("cfold", [16, 8], F32),
    ]
    for nm in ["inA", "inB", "inD", "inX", "tS1", "tS2", "tD", "tV1", "Abf", "Bbf", "zb"]:
        for p in range(2):
            blist.append((f"{nm}{p}", [16, CH], F32))
    for nm, sh, dt in blist:
        B.take(nm, sh, dt)

    total = max(A.off, B.off)
    assert total <= 196 * KB, f"SBUF budget blown: {total / KB:.1f} KB"

    sb = {}
    with nc.sbuf_tensor("arena", [128, total], U8) as arena:
        base_addr = nc.lookup_mloc(arena).addr
        for name, shape, dtype, off in pend:
            sb[name] = nc.alloc_sbuf_tensor_at(name, shape, dtype, offset=base_addr + off)
        _prog(nc, sb, dr)
    return nc


def _prog(nc, sb, dr):
    PH = _PHASES
    from contextlib import ExitStack
    with ExitStack() as st:
        P01 = st.enter_context(nc.psum_tensor("P01", [128, 1024], F32))
        P23 = st.enter_context(nc.psum_tensor("P23", [128, 1024], F32))
        P4 = st.enter_context(nc.psum_tensor("P4", [128, 512], F32))
        P5 = st.enter_context(nc.psum_tensor("P5", [128, 512], F32))
        P6 = st.enter_context(nc.psum_tensor("P6", [128, 512], F32))
        P7 = st.enter_context(nc.psum_tensor("P7", [128, 512], F32))

        def sem(name):
            return st.enter_context(nc.semaphore(name))

        QX = QSem(sem, "qx", 2)
        QWLD = QSem(sem, "qw", 3)
        QXND = QSem(sem, "qn", 2)
        QX1 = QSem(sem, "q1", 2)
        QST = QSem(sem, "qs", 2)
        QSS = QSem(sem, "qp", 3)
        QEB = QSem(sem, "qe", 2)
        QX2 = QSem(sem, "q2", 3)
        QXN2 = QSem(sem, "qq", 2)
        QOUT = QSem(sem, "qo", 2)

        c_lns, c_tr, c_evV, c_evS = sem("c_lns"), sem("c_tr"), sem("c_evV"), sem("c_evS")
        c_mmxz, c_evxz, c_cv, c_sil = sem("c_mmxz"), sem("c_evxz"), sem("c_cv"), sem("c_sil")
        c_dblm, c_evdbl, c_gate, c_evgate = sem("c_dblm"), sem("c_evdbl"), sem("c_gate"), sem("c_evgate")
        c_hin, c_sp, c_pp, c_scan = sem("c_hin"), sem("c_sp"), sem("c_pp"), sem("c_scan")
        c_smy, c_cc, c_gvv, c_gtf = sem("c_smy"), sem("c_cc"), sem("c_gvv"), sem("c_gtf")
        c_gse, c_gex = sem("c_gse"), sem("c_gex")
        c_gtb, c_wfm, c_hzc, c_mmG, c_mmE = sem("c_gtb"), sem("c_wfm"), sem("c_hzc"), sem("c_mmG"), sem("c_mmE")
        c_stt, c_stt0, c_mmM, c_madd = sem("c_stt"), sem("c_stt0"), sem("c_mmM"), sem("c_madd")
        block = st.enter_context(nc.Block())
        pxz = [P4, P5]
        pcv = [P6, P7]
        pdbl = P01[0:48, 0:512]
        pgate = P01[0:8, 512:1024]
        p23bf = P23.bitcast(BF16)
        ptr = [p23bf[:, 0:128], p23bf[:, 1024:1152]]
        pe = [P01, P23]
        pm = [P4, P5]
        pgf = P6[:, 0:8]
        pgb = P7[0:8, 0:128]

        TR = [("xn", 0, k) for k in range(8)]
        for i in range(1, NG):
            TR += [("xn", i, k) for k in range(8)]
            if i == 2:
                TR += [("wo16", 0, k) for k in range(8)]
            elif i == 3:
                TR += [("wdbl", 0, k) for k in range(8)]
            elif i == 4:
                TR += [("wdbl", 0, k) for k in range(8, 16)]
            elif 5 <= i <= 21:
                TR += [("wi", i - 5, k) for k in range(8)]
        nV = nS = 0
        plan = []
        last_tr_of_group = {}
        last_tr_of_wichunk = {}
        last_evV_of_group = {}
        for n, (kind, idx, k) in enumerate(TR):
            if kind == "xn":
                nV += 1
                ev = ("V", nV)
                last_tr_of_group[idx] = n + 1
                last_evV_of_group[idx] = nV
            else:
                nS += 1
                ev = ("S", nS)
                if kind == "wi":
                    last_tr_of_wichunk[idx] = n + 1
            plan.append((kind, idx, k, ev))
        N_EVS = nS

        def gtile(i):
            return (i - 1) // G1

        (W_IDBF, W_IDF, W_BI, W_BC, W_B48, W_GB, W_HM, W_CST, W_SEL,
         W_SELI, W_EB, W_GW, W_WO, W_WDBL, W_DGA, W_DGB) = range(16)

        def w_wi(r):
            return 16 + r

        W_TOT = w_wi(16) + 1

        def evxz_cnt(t, m):
            return (m + 1) if t < 0 else (16 + 17 * t + m + 1)

        def st_idx(t, j):
            return 4 * t + j

        SS_GF = 0

        def ss_chunk(c, j):
            return 1 + 4 * c + j

        SS_AGIN = 1 + 4 * NCH
        SS_GATH = 2 + 4 * NCH

        def ss_z(c):
            return 3 + 4 * NCH + c

        @block.sync
        def _(sy):
            def ldx(t):
                if t < 0:
                    QX.issue(sy, lambda: sy.dma_start(sb["xbuf0"][0:3, 0:D], dr["xh"][:, :]))
                    return
                slot = sb["xbuf1"] if t % 2 == 0 else sb["xbuf0"]
                if t == 1:
                    sy.wait_ge(c_lns, 1)
                elif t >= 2:
                    sy.wait_ge(c_lns, 4 * (t - 2) + 5)
                src = dr["x"][t * T1:(t + 1) * T1, :].rearrange("(g p) f -> p g f", p=128)
                QX.issue(sy, lambda: sy.dma_start(
                    slot[:, :].rearrange("p (g f) -> p g f", g=G1), src))

            ldx(-1)
            ldx(0)
            ldx(1)
            for t in range(NT1):
                sy.wait_ge(c_evV, last_evV_of_group[1 + 4 * t + 3])
                xnf = sb["xnf0"] if t % 2 == 0 else sb["xnf1"]
                dst = dr["xn_dram"][:, :, t * T1:(t + 1) * T1].rearrange("k p c -> p k c")
                QXND.issue(sy, lambda: sy.dma_start(dst, xnf[:, :, :]))
                if t + 2 < NT1:
                    ldx(t + 2)

            if PH == "0":
                QXND.wait_all(sy, NT1)
                if _DBG:
                    QOUT.issue(sy, lambda: sy.dma_start(dr["d_lnb"][:, :], sb["lnbuf"][:, :]))
                    QOUT.wait_all(sy, 1)
                return

            def stores(t):
                par = t % 2
                sy.wait_ge(c_sil, 16 * t + 1)
                QST.issue(sy, lambda: sy.dma_start(
                    dr["x16_dram"][:, t * T1:(t + 1) * T1], sb[f"xstg{par}"][:, :]))
                sy.wait_ge(c_evxz, evxz_cnt(t, 16))
                QST.issue(sy, lambda: sy.dma_start(
                    dr["z16_dram"][:, t * T1:(t + 1) * T1], sb[f"zstg{par}"][:, :]))
                sy.wait_ge(c_evdbl, t + 1)
                QST.issue(sy, lambda: sy.dma_start(
                    dr["dbl_dram"][:, t * T1:(t + 1) * T1], sb[f"dstg{par}"][:, :]))
                sy.wait_ge(c_evgate, t + 1)
                QST.issue(sy, lambda: sy.dma_start(
                    dr["gate_dram"][:, t * T1:(t + 1) * T1], sb[f"gstg{par}"][:, :]))

            if PH in _HALO_MODES:
                QXND.wait_all(sy, NT1)
                return
            NTX = _NT1X
            for t in range(NTX):
                if t >= 2:
                    stores(t - 2)
                QXND.wait_done(sy, t)
                if t >= 2:
                    sy.wait_ge(c_gate, t - 1)
                xn1 = sb["xn1b0"] if t % 2 == 0 else sb["xn1b1"]
                src = dr["xn_dram"][:, :, t * T1:(t + 1) * T1].rearrange("k p c -> p k c")
                QX1.issue(sy, lambda: sy.dma_start(xn1[:, :, :], src))
            if NTX >= 2:
                stores(NTX - 2)
            if NTX >= 1:
                stores(NTX - 1)
            if NTX < NT1:
                QST.wait_all(sy, 4 * NTX)
                return
            if PH == "01":
                QST.wait_all(sy, 4 * NT1)
                return
            sy.wait_ge(c_evgate, NT1)
            sy.wait_ge(c_gate, NT1)
            QST.wait_all(sy, 4 * NT1)
            QSS.issue(sy, lambda: sy.dma_start(sb["gfull"][:, :], dr["gate_dram"][:, :]))
            if PH == "full":
                src0 = dr["xn_dram"][:, :, 0:T2].rearrange("k p c -> p k c")
                QXN2.issue(sy, lambda: sy.dma_start(sb["xn2b0"][:, :, :], src0))
                for g in range(3):
                    QX2.issue(sy, lambda: sy.dma_start(
                        sb[f"x2r{g % 3}"][:, :], dr["x"][g * 128:(g + 1) * 128, :]))
            for c in range(NCH):
                p = c % 2
                if c >= 2:
                    sy.wait_ge(c_scan, 2 * (c - 2) + 2)
                QSS.issue(sy, lambda: sy.dma_start(
                    sb[f"inA{p}"][:, :], dr["dbl_dram"][0:16, c * CH:(c + 1) * CH]))
                QSS.issue(sy, lambda: sy.dma_start(
                    sb[f"inB{p}"][:, :], dr["dbl_dram"][16:32, c * CH:(c + 1) * CH]))
                QSS.issue(sy, lambda: sy.dma_start(
                    sb[f"inD{p}"][:, :], dr["dbl_dram"][32:48, c * CH:(c + 1) * CH]))
                QSS.issue(sy, lambda: sy.dma_start(
                    sb[f"inX{p}"][:, :], dr["x16_dram"][:, c * CH:(c + 1) * CH]))
            sy.wait_ge(c_smy, 2)
            QSS.issue(sy, lambda: sy.dma_start(dr["ag_in"][:, :], sb["smry"][:, :]))
            if PH == "full":
                for g in range(3, G2):
                    sy.wait_ge(c_stt0, g - 2)
                    QX2.issue(sy, lambda: sy.dma_start(
                        sb[f"x2r{g % 3}"][:, :], dr["x"][g * 128:(g + 1) * 128, :]))
            sy.wait_ge(c_smy, 3)
            QSS.issue(sy, lambda: sy.dma_start(
                sb["gath"][:, :].rearrange("p (r c) -> p r c", r=4),
                dr["ag_out"][:, :, :].rearrange("r p c -> p r c")))
            for c in range(NCH):
                if c >= 2:
                    sy.wait_ge(c_hzc, 2 * (c - 2) + 2)
                QSS.issue(sy, lambda: sy.dma_start(
                    sb[f"zb{c % 2}"][:, :], dr["z16_dram"][:, c * CH:(c + 1) * CH]))
            if PH == "01S":
                QSS.wait_all(sy, 3 + 5 * NCH)
                sy.wait_ge(c_hzc, 2 * NCH)
                return
            for t in range(1, NT2):
                sy.wait_ge(c_mmE, 8 * t)
                src2 = dr["xn_dram"][:, :, t * T2:(t + 1) * T2].rearrange("k p c -> p k c")
                QXN2.issue(sy, lambda: sy.dma_start(sb["xn2b0"][:, :, :], src2))
                sy.wait_ge(c_madd, 16 * (t - 1) + 16)
                dst1 = dr["out"][(t - 1) * T2:t * T2, :].rearrange("(g p) f -> p g f", p=128)
                QOUT.issue(sy, lambda: sy.dma_start(dst1, sb["acc"][:, :, :]))
                for g in range(G2):
                    gx = 8 * t + g
                    sy.wait_ge(c_stt0, max(0, gx - 2))
                    QX2.issue(sy, lambda: sy.dma_start(
                        sb[f"x2r{gx % 3}"][:, :],
                        dr["x"][t * T2 + g * 128:t * T2 + (g + 1) * 128, :]))
            sy.wait_ge(c_madd, 16 * (NT2 - 1) + 16)
            dst2 = dr["out"][(NT2 - 1) * T2:NT2 * T2, :].rearrange("(g p) f -> p g f", p=128)
            QOUT.issue(sy, lambda: sy.dma_start(dst2, sb["acc"][:, :, :]))
            QOUT.wait_all(sy, NT2)
            if _DBG:
                for nmA, nmB in [("d_wtm", "wtm"), ("d_wfm", "wfm"), ("d_hz", "hz_bf"),
                                 ("d_hloc", "h_loc"), ("d_cf", "cfold"), ("d_gath", "gath")]:
                    QOUT.issue(sy, lambda nmA=nmA, nmB=nmB: sy.dma_start(
                        dr[nmA][:, :], sb[nmB][:, :]))
                QOUT.wait_all(sy, NT2 + 6)

        @block.gpsimd
        def _(g):
            def ld(mk):
                QWLD.issue(g, mk)

            ld(lambda: g.dma_start(sb["idbf"][:, :], dr["idin"][:, :]))
            ld(lambda: g.dma_start(sb["idf32"][:, :], dr["idin"][:, :]))
            ld(lambda: g.dma_start(sb["bi_sb"][:, :], dr["bi"][:].rearrange("(m p) -> p m", p=128),
                                   allow_slow_non_contiguous=True))
            ld(lambda: g.dma_start(sb["bc_sb"][:, :], dr["bc"][:].rearrange("(m p) -> p m", p=128),
                                   allow_slow_non_contiguous=True))
            ld(lambda: g.dma_start(sb["b48_sb"][:, :], dr["b48"][:, :]))
            ld(lambda: g.dma_start(sb["gb_sb"][:, :], dr["gb"][:, :]))
            ld(lambda: g.dma_start(sb["hm_sb"][:, :], dr["hmask"][:, :]))
            ld(lambda: g.dma_start(sb["cst_sb"][:, :], dr["cst"][:, :]))
            ld(lambda: g.dma_start(sb["sel_sb"][:, :], dr["sel"][:, :]))
            ld(lambda: g.dma_start(sb["seli_sb"][:, :], dr["selinv"][:, :]))
            ld(lambda: g.dma_start(sb["eb_sb"][:, :], dr["eb"][:, :]))
            ld(lambda: g.dma_start(sb["gw_sb"][:, :, :],
                                   dr["gw"][:, :].rearrange("(k p) e -> p k e", p=128)))
            ld(lambda: g.dma_start(sb["wo16st"][:, :].rearrange("p (k s) -> p k s", k=8),
                                   dr["wo16"][:, :].rearrange("(k p) s -> p k s", p=128)))
            ld(lambda: g.dma_start(sb["wdblst"][:, :], dr["wdbl"][:, :]))
            ld(lambda: g.dma_start(sb["diag_sb"][:, 0:8, :, :],
                                   dr["wcdiag"][0:8, :, :, :].rearrange("m t i j -> i m t j")))
            ld(lambda: g.dma_start(sb["diag_sb"][:, 8:16, :, :],
                                   dr["wcdiag"][8:16, :, :, :].rearrange("m t i j -> i m t j")))
            for r in range(17):
                if r >= 2:
                    g.wait_ge(c_tr, last_tr_of_wichunk[r - 2])
                stg = sb["wstage0"] if r % 2 == 0 else sb["wstage1"]
                if r < 16:
                    ld(lambda: g.dma_start(stg[:, :], dr["wi"][r * 128:(r + 1) * 128, :]))
                else:
                    ld(lambda: g.dma_start(stg[0:16, :], dr["wi"][2048:2064, :]))
            if PH in ("0", "01") or PH in _HALO_MODES or _NT1X < NT1:
                return
            g.wait_ge(c_gate, NT1)
            g.wait_ge(c_evgate, NT1)
            QXND.wait_all(g, NT1)
            QST.wait_all(g, 4 * NT1)
            g.wait_ge(c_evxz, evxz_cnt(NT1 - 1, 16))
            g.wait_ge(c_sil, 16 * NT1)

            def eb_load(idx):
                if idx >= 2:
                    g.wait_ge(c_mmE, idx - 1)
                eb = sb["ebuf0"] if idx % 2 == 0 else sb["ebuf1"]
                e = idx % NE
                src = dr["ew"][e, :, :].rearrange("(k p) n -> p k n", p=128)
                QEB.issue(g, lambda: g.dma_start(eb[:, :, :], src))

            def chunk_pool(c):
                p = c % 2
                if c >= 2:
                    g.wait_ge(c_sp, 5 * (c - 2) + 5)
                QSS.wait_done(g, ss_chunk(c, 2))
                g.tensor_single_scalar(sb[f"tV1{p}"][:, :], sb[f"inD{p}"][:, :], 0.0, ALU.max)
                g.drain().then_inc(c_pp, 1)
                g.wait_ge(c_sp, 5 * c + 3)
                g.tensor_add(sb[f"tD{p}"][:, :], sb[f"tS2{p}"][:, :], sb[f"tV1{p}"][:, :])
                g.drain().then_inc(c_pp, 1)
                g.wait_ge(c_sp, 5 * c + 4)
                g.tensor_mul(sb[f"tV1{p}"][:, :], sb[f"tS1{p}"][:, :], sb[f"tD{p}"][:, :])
                g.drain().then_inc(c_pp, 1)
                QSS.wait_done(g, ss_chunk(c, 1))
                g.tensor_mul(sb[f"tS2{p}"][:, :], sb[f"inB{p}"][:, :], sb[f"tD{p}"][:, :])
                g.drain().then_inc(c_pp, 1)
                QSS.wait_done(g, ss_chunk(c, 3))
                g.tensor_mul(sb[f"Bbf{p}"][:, :], sb[f"tS2{p}"][:, :], sb[f"inX{p}"][:, :])
                g.drain().then_inc(c_pp, 1)

            if PH == "full":
                eb_load(0)
                eb_load(1)
            for c in range(NCH):
                chunk_pool(c)
            if PH == "01S":
                QSS.wait_done(g, SS_AGIN)
                g.collective_compute(
                    "AllGather", ALU.bypass,
                    replica_groups=[[0, 1, 2, 3], [4, 5, 6, 7]],
                    ins=[dr["ag_in"][:, :].opt()],
                    outs=[dr["ag_out"][:, :, :].opt()],
                ).then_inc(c_cc, 1)
                return
            for idx in range(2, NT2 * NE):
                eb_load(idx)
                if idx == 4:
                    QSS.wait_done(g, SS_AGIN)
                    g.collective_compute(
                        "AllGather", ALU.bypass,
                        replica_groups=[[0, 1, 2, 3], [4, 5, 6, 7]],
                        ins=[dr["ag_in"][:, :].opt()],
                        outs=[dr["ag_out"][:, :, :].opt()],
                    ).then_inc(c_cc, 1)

        def _dblmm(t_, t, m):
            t_.wait_ge(c_sil, 16 * t + m + 1)
            if m == 0:
                t_.wait_ge(c_evdbl, t)
            xc = sb["xcb0"] if m % 2 == 0 else sb["xcb1"]
            t_.matmul(pdbl[:, :], sb["wdbl_lhsT"][:, m, :], xc[:, :],
                      start=(m == 0), stop=(m == 15))
            t_.drain().then_inc(c_dblm, 1)

        @block.tensor
        def _(t_):
            for n, (kind, idx, k, ev) in enumerate(plan):
                if n >= 2:
                    pev = plan[n - 2][3]
                    t_.wait_ge(c_evV if pev[0] == "V" else c_evS, pev[1])
                dst = ptr[n % 2]
                if kind == "xn":
                    if k == 0:
                        t_.wait_ge(c_lns, idx + 1)
                    src_t = sb["xn_tm0"] if idx % 2 == 0 else sb["xn_tm1"]
                    rows = 3 if idx == 0 else 128
                    t_.transpose(dst[0:128, 0:rows], src_t[0:rows, k * 128:(k + 1) * 128],
                                 sb["idbf"][0:rows, 0:rows]).then_inc(c_tr, 1)
                elif kind == "wi":
                    if k == 0:
                        QWLD.wait_done(t_, w_wi(idx))
                    stg = sb["wstage0"] if idx % 2 == 0 else sb["wstage1"]
                    rows = 128 if idx < 16 else 16
                    t_.transpose(dst[0:128, 0:rows], stg[0:rows, k * 128:(k + 1) * 128],
                                 sb["idbf"][0:rows, 0:rows]).then_inc(c_tr, 1)
                elif kind == "wdbl":
                    if k in (0, 8):
                        QWLD.wait_done(t_, W_WDBL)
                    t_.transpose(dst[0:128, 0:48], sb["wdblst"][0:48, k * 128:(k + 1) * 128],
                                 sb["idbf"][0:48, 0:48]).then_inc(c_tr, 1)
                else:
                    if k == 0:
                        QWLD.wait_done(t_, W_WO)
                    t_.transpose(dst[0:16, 0:128], sb["wo16st"][:, k * 16:(k + 1) * 16],
                                 sb["idbf"][:, :]).then_inc(c_tr, 1)

            if PH == "0":
                return
            QWLD.wait_all(t_, W_TOT)
            t_.wait_ge(c_evS, N_EVS)
            t_.wait_ge(c_evV, 8 * NG)
            for m in range(16):
                if m >= 2 and PH != "Hm":
                    t_.wait_ge(c_evxz, m - 1)
                ps = pxz[m % 2]
                for k in range(8):
                    t_.matmul(ps[:, 0:3], sb["wi_lhsT"][:, k, m * 128:(m + 1) * 128],
                              sb["xnh_fm"][:, k, :], start=(k == 0), stop=(k == 7))
                t_.drain().then_inc(c_mmxz, 1)

            if PH in _HALO_MODES:
                return
            for t in range(_NT1X):
                QX1.wait_done(t_, t)
                xn1 = sb["xn1b0"] if t % 2 == 0 else sb["xn1b1"]
                for m in range(17):
                    u = 17 * t + m
                    t_.wait_ge(c_evxz, evxz_cnt(t, m) - 2)
                    ps = pxz[(16 + u) % 2]
                    rows = 128 if m < 16 else 16
                    for k in range(8):
                        t_.matmul(ps[0:rows, :], sb["wi_lhsT"][:, k, m * 128:m * 128 + rows],
                                  xn1[:, k, :], start=(k == 0), stop=(k == 7))
                    t_.drain().then_inc(c_mmxz, 1)
                    if m >= 1:
                        mm = m - 1
                        uc = 16 * t + mm
                        if t == 0 and mm < 2:
                            QWLD.wait_done(t_, W_DGA)
                            QWLD.wait_done(t_, W_DGB)
                        t_.wait_ge(c_sil, max(0, uc - 1))
                        t_.wait_ge(c_evxz, evxz_cnt(t, mm))
                        t_.wait_ge(c_hin, uc + 1)
                        pc = pcv[uc % 2]
                        cb = sb["cbuf0"] if mm % 2 == 0 else sb["cbuf1"]
                        for tap in range(4):
                            t_.matmul(pc[:, :], sb["diag_sb"][:, mm, tap, :],
                                      cb[:, tap:tap + 512], start=(tap == 0), stop=(tap == 3))
                        t_.drain().then_inc(c_cv, 1)
                        if mm >= 1:
                            _dblmm(t_, t, mm - 1)
                _dblmm(t_, t, 15)
                t_.wait_ge(c_evgate, t)
                for k in range(8):
                    t_.matmul(pgate[:, :], sb["gw_sb"][:, k, :], xn1[:, k, :],
                              start=(k == 0), stop=(k == 7))
                t_.drain().then_inc(c_gate, 1)

            if PH in ("01",) or _NT1X < NT1:
                return
            t_.wait_ge(c_evdbl, NT1)
            t_.wait_ge(c_evgate, NT1)
            t_.wait_ge(c_evxz, evxz_cnt(NT1 - 1, 16))
            t_.wait_ge(c_sil, 16 * NT1)
            QSS.wait_done(t_, SS_GF)

            for gg in range(32):
                if gg >= 1:
                    t_.wait_ge(c_gse, gg)
                t_.transpose(pgf[:, :], sb["gfull"][:, gg * 128:(gg + 1) * 128],
                             sb["idf32"][0:8, 0:8])
                t_.drain().then_inc(c_gtf, 1)
                t_.wait_ge(c_gvv, 8 * gg + 8)
                if gg >= 1:
                    t_.wait_ge(c_wfm, gg)
                t_.transpose(pgb[:, :], sb["wtm"][:, 8 * gg:8 * (gg + 1)],
                             sb["idf32"][:, :])
                t_.drain().then_inc(c_gtb, 1)
            if PH == "01S":
                return
            for t in range(NT2):
                QXN2.wait_done(t_, t)
                xn2 = sb["xn2b0"]
                for e in range(NE):
                    QEB.wait_done(t_, 8 * t + e)
                    eb = sb["ebuf0"] if (8 * t + e) % 2 == 0 else sb["ebuf1"]
                    for g_ in range(G2):
                        geidx = t * 64 + e * 8 + g_
                        pp = pe[geidx % 2]
                        for nh in range(2):
                            if geidx >= 2:
                                t_.wait_ge(c_stt, 2 * (geidx - 2) + nh + 1)
                            for k in range(8):
                                t_.matmul(pp[:, nh * 512:(nh + 1) * 512],
                                          xn2[:, k, g_ * 128:(g_ + 1) * 128],
                                          eb[:, k, nh * 512:(nh + 1) * 512],
                                          start=(k == 0), stop=(k == 7))
                            t_.drain().then_inc(c_mmG, 1)
                    t_.drain().then_inc(c_mmE, 1)
                if t == 0:
                    t_.wait_ge(c_hzc, 2 * NCH)
                    t_.wait_ge(c_wfm, 32)
                for g_ in range(G2):
                    for nh in range(2):
                        mcnt = 16 * t + 2 * g_ + nh + 1
                        t_.wait_ge(c_madd, max(0, mcnt - 2))
                        pmx = pm[(2 * g_ + nh) % 2]
                        c0 = t * T2 + g_ * 128
                        t_.matmul(pmx[:, :], sb["hz_bf"][:, c0:c0 + 128],
                                  sb["wo16m"][:, nh * 512:(nh + 1) * 512],
                                  start=True, stop=False)
                        t_.matmul(pmx[:, :], sb["wfm"][:, c0:c0 + 128],
                                  sb["eb_sb"][:, nh * 512:(nh + 1) * 512],
                                  start=False, stop=True)
                        t_.drain().then_inc(c_mmM, 1)

        @block.vector
        def _(v):
            v = _DrainingEngine(v)

            for n, (kind, idx, k, ev) in enumerate(plan):
                if ev[0] != "V":
                    continue
                i = idx
                rows = 3 if i == 0 else 128
                if i >= 1:
                    t = gtile(i)
                    g = (i - 1) % G1
                    if g == 0 and k == 0 and t >= 2:
                        QXND.wait_done(v, t - 2)
                    dstb = sb["xnf0"] if t % 2 == 0 else sb["xnf1"]
                    dstp = dstb[:, k, g * 128:g * 128 + rows]
                else:
                    dstp = sb["xnh_fm"][:, k, :]
                v.wait_ge(c_tr, n + 1)
                v.tensor_copy(dstp, ptr[n % 2][0:128, 0:rows]).then_inc(c_evV, 1)

            if PH in ("0", "01") or PH in _HALO_MODES or _NT1X < NT1:
                return
            v.wait_ge(c_evgate, NT1)
            v.wait_ge(c_gate, NT1)
            QST.wait_all(v, 4 * NT1)
            v.memset(sb["zeros16"][:, :], 0.0)

            def scan_chunk(c):
                p = c % 2
                v.wait_ge(c_sp, 5 * c + 5)
                v.wait_ge(c_pp, 5 * c + 5)
                v.nop(cycle_cnt=128)
                ini_h = 0.0 if c == 0 else sb["h_loc"][:, c * CH - 1:c * CH]
                ini_p = 1.0 if c == 0 else sb["cumP"][:, c * CH - 1:c * CH]
                v.tensor_tensor_scan(sb["h_loc"][:, c * CH:(c + 1) * CH], sb[f"Abf{p}"][:, :],
                                     sb[f"Bbf{p}"][:, :], ini_h, ALU.mult, ALU.add).then_inc(c_scan, 1)
                v.tensor_tensor_scan(sb["cumP"][:, c * CH:(c + 1) * CH], sb[f"Abf{p}"][:, :],
                                     sb["zeros16"][:, :], ini_p, ALU.mult, ALU.add).then_inc(c_scan, 1)

            for gg in range(32):
                v.wait_ge(c_gex, gg + 1)
                v.reduce_sum(sb["sft"][:, 1:2], sb["esb"][:, :], axis=AX.X).then_inc(c_gvv, 1)
                v.reciprocal(sb["sft"][:, 2:3], sb["sft"][:, 1:2]).then_inc(c_gvv, 1)
                v.nop(cycle_cnt=192)
                v.tensor_single_scalar(sb["psb"][:, :], sb["esb"][:, :], sb["sft"][:, 2:3], ALU.mult).then_inc(c_gvv, 1)
                v.tensor_single_scalar(sb["msk"][:, :], sb["psb"][:, :], 0.1, ALU.is_ge).then_inc(c_gvv, 1)
                v.tensor_mul(sb["psb"][:, :], sb["psb"][:, :], sb["msk"][:, :]).then_inc(c_gvv, 1)
                v.reduce_sum(sb["sft"][:, 1:2], sb["psb"][:, :], axis=AX.X).then_inc(c_gvv, 1)
                v.reciprocal(sb["sft"][:, 2:3], sb["sft"][:, 1:2]).then_inc(c_gvv, 1)
                v.nop(cycle_cnt=192)
                v.tensor_single_scalar(sb["wtm"][:, 8 * gg:8 * (gg + 1)], sb["psb"][:, :],
                                       sb["sft"][:, 2:3], ALU.mult).then_inc(c_gvv, 1)
                if gg % 2 == 1:
                    scan_chunk(gg // 2)

            def stts(t, e):
                for g_ in range(G2):
                    geidx = t * 64 + e * 8 + g_
                    gx = 8 * t + g_
                    pp = pe[geidx % 2]
                    for nh in range(2):
                        v.wait_ge(c_mmG, 2 * geidx + nh + 1)
                        accsl = sb["acc"][:, g_, nh * 512:(nh + 1) * 512]
                        if e == 0:
                            QX2.wait_done(v, gx)
                            in1 = sb[f"x2r{gx % 3}"][:, nh * 512:(nh + 1) * 512]
                        else:
                            in1 = accsl
                        wcol = sb["wtm"][:, gx * 8 + e:gx * 8 + e + 1]
                        v.scalar_tensor_tensor(accsl, pp[:, nh * 512:(nh + 1) * 512],
                                               wcol, in1, ALU.mult, ALU.add).then_inc(c_stt, 1)
                        if e == 0 and nh == 1:
                            v.nop().then_inc(c_stt0, 1)

            def madds(t):
                for g_ in range(G2):
                    for nh in range(2):
                        v.wait_ge(c_mmM, 16 * t + 2 * g_ + nh + 1)
                        accsl = sb["acc"][:, g_, nh * 512:(nh + 1) * 512]
                        v.tensor_add(accsl, accsl, pm[(2 * g_ + nh) % 2][:, :]).then_inc(c_madd, 1)

            v.tensor_copy(sb["smry"][:, 0:1], sb["cumP"][:, TOK - 1:TOK]).then_inc(c_smy, 1)
            v.tensor_copy(sb["smry"][:, 1:2], sb["h_loc"][:, TOK - 1:TOK]).then_inc(c_smy, 1)
            if PH == "full":
                stts(0, 0)
                stts(0, 1)
                stts(0, 2)
                stts(0, 3)
            v.wait_ge(c_cc, 1)
            v.nop().then_inc(c_smy, 1)
            QSS.wait_done(v, SS_GATH)
            QWLD.wait_done(v, W_SEL)
            QWLD.wait_done(v, W_SELI)
            C = sb["cfold"]
            v.memset(C[:, 0:1], 0.0)
            v.nop(cycle_cnt=512)
            for j in range(3):
                v.tensor_single_scalar(C[:, 2:3], sb["gath"][:, 2 * j:2 * j + 1], sb["sel_sb"][:, j:j + 1], ALU.mult)
                v.nop(cycle_cnt=512)
                v.tensor_add(C[:, 2:3], C[:, 2:3], sb["seli_sb"][:, j:j + 1])
                v.nop(cycle_cnt=512)
                v.tensor_single_scalar(C[:, 3:4], sb["gath"][:, 2 * j + 1:2 * j + 2], sb["sel_sb"][:, j:j + 1], ALU.mult)
                v.nop(cycle_cnt=512)
                v.scalar_tensor_tensor(C[:, (j + 1) % 2:(j + 1) % 2 + 1], C[:, j % 2:j % 2 + 1],
                                       C[:, 2:3], C[:, 3:4], ALU.mult, ALU.add)
                v.nop(cycle_cnt=512)
            carry = C[:, 4:5]
            v.tensor_copy(carry, C[:, 1:2])
            v.nop(cycle_cnt=512)
            for c in range(NCH):
                cols = slice(c * CH, (c + 1) * CH)
                v.scalar_tensor_tensor(sb["h_loc"][:, cols], sb["cumP"][:, cols], carry,
                                       sb["h_loc"][:, cols], ALU.mult, ALU.add).then_inc(c_hzc, 1)
                QSS.wait_done(v, ss_z(c))
                v.tensor_mul(sb["hz_bf"][:, cols], sb["h_loc"][:, cols], sb[f"zb{c % 2}"][:, :]).then_inc(c_hzc, 1)
                if PH == "full":
                    if c == 3:
                        stts(0, 4)
                    elif c == 7:
                        stts(0, 5)
                    elif c == 11:
                        stts(0, 6)
                    elif c == 15:
                        stts(0, 7)
            if PH == "01S":
                return
            madds(0)
            for t in range(1, NT2):
                QOUT.wait_done(v, t - 1)
                for e in range(NE):
                    stts(t, e)
                madds(t)

        @block.scalar
        def _(s):
            s = _DrainingEngine(s)

            def ln_s(i):
                rows = 3 if i == 0 else 128
                if i == 0:
                    xg = sb["xbuf0"][0:3, 0:D]
                    QX.wait_done(s, 0)
                else:
                    t = gtile(i)
                    g = (i - 1) % G1
                    xb = sb["xbuf1"] if t % 2 == 0 else sb["xbuf0"]
                    xg = xb[:, g * D:(g + 1) * D]
                    QX.wait_done(s, t + 1)
                L = sb["lnbuf"]
                cst = sb["cst_sb"]
                o = (i % 2) * 8

                def Lc(c):
                    return L[0:rows, o + c:o + c + 1]

                s.activation(sb["sqdump"][0:rows, :], xg, AF.Identity, accum_out=Lc(0))
                s.activation(sb["sqdump"][0:rows, :], xg, AF.Square, accum_out=Lc(1))
                s.activation(Lc(2), Lc(0), AF.Identity, scale=cst[0:rows, 0:1])
                s.activation(Lc(3), Lc(1), AF.Identity, scale=cst[0:rows, 0:1],
                             bias=cst[0:rows, 1:2])
                s.activation(Lc(4), Lc(2), AF.Identity, scale=-1.0)
                s.activation(Lc(5), Lc(2), AF.Identity, scale=Lc(2))
                s.activation(Lc(6), Lc(5), AF.Identity, scale=-1.0, bias=Lc(3))
                s.activation(Lc(7), Lc(6), AF.Ln)
                s.activation(Lc(5), Lc(7), AF.Exp, scale=cst[0:rows, 2:3])
                if i >= 2:
                    s.wait_ge(c_tr, last_tr_of_group[i - 2])
                xnt = sb["xn_tm0"] if i % 2 == 0 else sb["xn_tm1"]
                s.activation(sb["sqdump"][0:rows, :], xg, AF.Identity, bias=Lc(4))
                s.activation(xnt[0:rows, :], sb["sqdump"][0:rows, :], AF.Identity,
                             scale=Lc(5)).then_inc(c_lns, 1)

            QWLD.wait_done(s, W_CST)
            ln_s(0)
            ln_s(1)
            for n, (kind, idx, k, ev) in enumerate(plan):
                if kind == "xn" and k == 0 and 1 <= idx and idx + 1 < NG:
                    ln_s(idx + 1)
                if ev[0] != "S":
                    continue
                s.wait_ge(c_tr, n + 1)
                if kind == "wi":
                    rows = 128 if idx < 16 else 16
                    s.activation(sb["wi_lhsT"][:, k, idx * 128:idx * 128 + rows],
                                 ptr[n % 2][0:128, 0:rows], AF.Copy).then_inc(c_evS, 1)
                elif kind == "wdbl":
                    s.activation(sb["wdbl_lhsT"][:, k, :], ptr[n % 2][0:128, 0:48],
                                 AF.Copy).then_inc(c_evS, 1)
                else:
                    s.activation(sb["wo16m"][:, k * 128:(k + 1) * 128], ptr[n % 2][0:16, 0:128],
                                 AF.Copy).then_inc(c_evS, 1)

            if PH in ("0", "Hm"):
                return
            QWLD.wait_done(s, W_BI)
            for m in range(16):
                s.wait_ge(c_mmxz, m + 1)
                if PH == "He":
                    s.activation(sb["tmp48"][:, 3 * m:3 * m + 3], pxz[m % 2][:, 0:3],
                                 AF.Identity).then_inc(c_evxz, 1)
                else:
                    s.activation(sb["tmp48"][:, 3 * m:3 * m + 3], pxz[m % 2][:, 0:3], AF.Identity,
                                 bias=sb["bi_sb"][:, m:m + 1]).then_inc(c_evxz, 1)
            if PH in ("He", "Heb"):
                return
            QWLD.wait_done(s, W_HM)
            s.activation(sb["haloxz"][:, :], sb["tmp48"][:, :], AF.Identity,
                         scale=sb["hm_sb"][:, 0:1])

            if PH == "H":
                return
            QWLD.wait_done(s, W_BC)
            QWLD.wait_done(s, W_B48)
            QWLD.wait_done(s, W_GB)
            for t in range(_NT1X):
                par = t % 2
                for m in range(17):
                    ps = pxz[(16 + 17 * t + m) % 2]
                    if m < 16:
                        u = 16 * t + m
                        cb = sb["cbuf0"] if m % 2 == 0 else sb["cbuf1"]
                        s.wait_ge(c_cv, max(0, u - 1))
                        s.activation(cb[:, 0:3], sb["haloxz"][:, 3 * m:3 * m + 3],
                                     AF.Identity).then_inc(c_hin, 1)
                        s.wait_ge(c_mmxz, evxz_cnt(t, m))
                        s.activation(cb[:, 3:515], ps[:, :], AF.Identity,
                                     bias=sb["bi_sb"][:, m:m + 1]).then_inc(c_evxz, 1)
                        s.activation(sb["haloxz"][:, 3 * m:3 * m + 3], cb[:, 512:515],
                                     AF.Identity)
                    else:
                        s.wait_ge(c_mmxz, evxz_cnt(t, m))
                        if t >= 2:
                            QST.wait_done(s, st_idx(t - 2, 1))
                        s.activation(sb[f"zstg{par}"][:, :], ps[0:16, :], AF.Identity,
                                     bias=sb["bi_sb"][0:16, 16:17]).then_inc(c_evxz, 1)
                    if m >= 1:
                        mm = m - 1
                        s.wait_ge(c_cv, 16 * t + mm + 1)
                        s.wait_ge(c_dblm, max(0, 16 * t + mm - 1))
                        pc = pcv[(16 * t + mm) % 2]
                        xc = sb["xcb0"] if mm % 2 == 0 else sb["xcb1"]
                        ins = s.activation(xc[:, :], pc[:, :], AF.Silu, bias=sb["bc_sb"][:, mm:mm + 1])
                        if mm == 0:
                            if t >= 2:
                                QST.wait_done(s, st_idx(t - 2, 0))
                            s.activation(sb[f"xstg{par}"][:, :], pc[0:16, :], AF.Silu,
                                         bias=sb["bc_sb"][0:16, 0:1]).then_inc(c_sil, 1)
                        else:
                            ins.then_inc(c_sil, 1)
                s.wait_ge(c_dblm, 16 * t + 16)
                if t >= 2:
                    QST.wait_done(s, st_idx(t - 2, 2))
                s.activation(sb[f"dstg{par}"][:, :], pdbl[:, :], AF.Identity,
                             bias=sb["b48_sb"][:, 0:1]).then_inc(c_evdbl, 1)
                s.wait_ge(c_gate, t + 1)
                if t >= 2:
                    QST.wait_done(s, st_idx(t - 2, 3))
                s.activation(sb[f"gstg{par}"][:, :], pgate[:, :], AF.Identity,
                             bias=sb["gb_sb"][:, 0:1]).then_inc(c_evgate, 1)

            if PH == "01" or _NT1X < NT1:
                return

            def chunk_acts(c):
                p = c % 2
                if c >= 2:
                    s.wait_ge(c_pp, 5 * (c - 2) + 5)
                QSS.wait_done(s, ss_chunk(c, 2))
                s.activation(sb[f"tS2{p}"][:, :], sb[f"inD{p}"][:, :], AF.Abs).then_inc(c_sp, 1)
                s.activation(sb[f"tS2{p}"][:, :], sb[f"tS2{p}"][:, :], AF.Exp, scale=-1.0).then_inc(c_sp, 1)
                s.activation(sb[f"tS2{p}"][:, :], sb[f"tS2{p}"][:, :], AF.Ln, bias=1.0).then_inc(c_sp, 1)
                QSS.wait_done(s, ss_chunk(c, 0))
                s.activation(sb[f"tS1{p}"][:, :], sb[f"inA{p}"][:, :], AF.Abs).then_inc(c_sp, 1)
                s.wait_ge(c_pp, 5 * c + 3)
                s.activation(sb[f"Abf{p}"][:, :], sb[f"tV1{p}"][:, :], AF.Exp, scale=-1.0).then_inc(c_sp, 1)

            QST.wait_all(s, 4 * NT1)
            s.activation(sb["sftn"][:, 0:1], sb["cst_sb"][:, 0:1], AF.Identity, scale=0.0)
            for gg in range(32):
                if gg >= 2:
                    s.wait_ge(c_gtb, gg - 1)
                    s.activation(sb["wfm"][0:8, 128 * (gg - 2):128 * (gg - 1)], pgb[:, :],
                                 AF.Copy).then_inc(c_wfm, 1)
                s.wait_ge(c_gtf, gg + 1)
                s.activation(sb["glog"][:, :], pgf[:, :], AF.Identity).then_inc(c_gse, 1)
                s.activation(sb["gmx"][:, :], sb["glog"][:, :], AF.Identity,
                             scale=sb["cst_sb"][:, 3:4], accum_out=sb["sftn"][:, 0:1])
                s.activation(sb["esb"][:, :], sb["glog"][:, :], AF.Exp,
                             bias=sb["sftn"][:, 0:1]).then_inc(c_gex, 1)
                if gg % 2 == 1:
                    chunk_acts(gg // 2)
            s.wait_ge(c_gtb, 31)
            s.activation(sb["wfm"][0:8, 128 * 30:128 * 31], pgb[:, :], AF.Copy).then_inc(c_wfm, 1)
            s.wait_ge(c_gtb, 32)
            s.activation(sb["wfm"][0:8, 128 * 31:128 * 32], pgb[:, :], AF.Copy).then_inc(c_wfm, 1)

    return nc


def kernel(L0_attn_x, L0_attn_w_in, L0_attn_b_in, L0_attn_w_conv, L0_attn_b_conv,
           L0_attn_w_xproj, L0_attn_w_dt, L0_attn_b_dt, L0_attn_w_out,
           L0_ffn_gate_w, L0_ffn_gate_b, L0_ffn_expert_w, L0_ffn_expert_b):
    x = np.ascontiguousarray(np.asarray(L0_attn_x, dtype=np.float32))
    w_in = np.asarray(L0_attn_w_in, dtype=np.float32)
    b_in = np.asarray(L0_attn_b_in, dtype=np.float32)
    w_conv = np.asarray(L0_attn_w_conv, dtype=np.float32)
    b_conv = np.ascontiguousarray(np.asarray(L0_attn_b_conv, dtype=np.float32))
    w_xproj = np.asarray(L0_attn_w_xproj, dtype=np.float32)
    w_dt = np.asarray(L0_attn_w_dt, dtype=np.float32)
    b_dt = np.asarray(L0_attn_b_dt, dtype=np.float32)
    w_out = np.asarray(L0_attn_w_out, dtype=np.float32)
    gate_w = np.ascontiguousarray(np.asarray(L0_ffn_gate_w, dtype=np.float32))
    gate_b = np.asarray(L0_ffn_gate_b, dtype=np.float32)
    expert_w = np.ascontiguousarray(np.asarray(L0_ffn_expert_w, dtype=np.float32))
    expert_b = np.ascontiguousarray(np.asarray(L0_ffn_expert_b, dtype=np.float32))

    if "nc" not in _CACHE:
        _CACHE["nc"] = _build()
    nc = _CACHE["nc"]

    wi = np.ascontiguousarray(w_in[0:2064, :])
    bi = np.zeros(2176, np.float32)
    bi[0:2064] = b_in[0:2064]
    wc = w_conv[:, 0, :]
    diag = np.zeros((16, 4, 128, 128), np.float32)
    ii = np.arange(128)
    for m in range(16):
        for tap in range(4):
            diag[m, tap, ii, ii] = wc[m * 128:(m + 1) * 128, tap]
    wdbl = np.ascontiguousarray(np.concatenate([w_xproj, w_dt], axis=0))
    b48 = np.ascontiguousarray(np.concatenate([np.zeros(32, np.float32), b_dt]).reshape(48, 1))
    wo16 = np.ascontiguousarray(w_out[:, 0:16])
    gb2 = np.ascontiguousarray(gate_b.reshape(8, 1))
    idin = np.eye(128, dtype=np.float32)
    cst = np.tile(np.array([[1.0 / D, EPS, -0.5, -0.125]], np.float32), (128, 1))

    x4 = x.reshape(2, 4, 4096, 1024)
    in_maps = []
    for core in range(8):
        seq, q = core // 4, core % 4
        xs = np.ascontiguousarray(x4[seq, q])
        if q == 0:
            xhs = np.zeros((3, 1024), np.float32)
            hm = np.zeros((128, 1), np.float32)
        else:
            xhs = np.ascontiguousarray(x[seq, q * 4096 - 3:q * 4096, :])
            hm = np.ones((128, 1), np.float32)
        sel = np.zeros((16, 3), np.float32)
        sel[:, 0:q] = 1.0
        selinv = np.ascontiguousarray((1.0 - sel).astype(np.float32))
        in_maps.append(dict(
            x=xs, xh=xhs, hmask=hm, cst=cst, sel=sel, selinv=selinv,
            wi=wi, bi=bi, wcdiag=diag, bc=b_conv, wdbl=wdbl, b48=b48,
            wo16=wo16, gw=gate_w, gb=gb2, ew=expert_w, eb=expert_b, idin=idin,
        ))
    res = run_bass_kernel_spmd(nc, in_maps, core_ids=list(range(8)))
    _CACHE["res"] = res
    out = np.empty((2, 16384, 1024), np.float32)
    for core in range(8):
        seq, q = core // 4, core % 4
        out[seq, q * 4096:(q + 1) * 4096, :] = res.results[core]["out"]
    return out


# version 05
# speedup vs baseline: 1.0271x; 1.0271x over previous
import numpy as np
import concourse.bass as bass
import concourse.mybir as mybir
from concourse.bass_utils import run_bass_kernel_spmd

F32 = mybir.dt.float32
BF16 = mybir.dt.bfloat16
U8 = mybir.dt.uint8
AF = mybir.ActivationFunctionType
ALU = mybir.AluOpType
AX = mybir.AxisListType

TOK = 4096
D = 1024
DI = 2048
DS = 16
NE = 8
T1 = 512
NT1 = TOK // T1
G1 = T1 // 128
NG = 1 + NT1 * G1
T2 = 1024
NT2 = TOK // T2
G2 = T2 // 128
CH = 256
NCH = TOK // CH
EPS = 1e-5
KB = 1024

_CACHE = {}
_DBG = False
_PHASES = "full"
_HALO_MODES = ("H", "Hm", "He", "Heb")
_NT1X = 8


class _DrainInst:

    def __init__(self, eng, ins):
        self._eng = eng
        self._ins = ins

    def then_inc(self, sem_, k=1):
        self._eng.drain().then_inc(sem_, k)
        return self


class _DrainingEngine:
    _RAW = {"wait_ge", "drain", "nop", "dma_start", "then_inc"}

    def __init__(self, eng):
        self._e = eng

    def __getattr__(self, name):
        f = getattr(self._e, name)
        if name in _DrainingEngine._RAW or not callable(f):
            return f

        def wrap(*a, **kw):
            return _DrainInst(self._e, f(*a, **kw))

        return wrap


class QSem:

    def __init__(self, mk, name, nslot):
        self.s = [mk(f"{name}{i}") for i in range(nslot)]
        self.nuse = [0] * nslot
        self.total = 0

    def issue(self, eng, mkdma):
        j = self.total
        sl = j % len(self.s)
        if self.nuse[sl] > 0:
            eng.wait_ge(self.s[sl], 16 * self.nuse[sl])
        mkdma().then_inc(self.s[sl], 16)
        self.nuse[sl] += 1
        self.total += 1
        return j

    def wait_done(self, eng, j):
        assert j >= 0
        eng.wait_ge(self.s[j % len(self.s)], 16 * (j // len(self.s) + 1))

    def wait_all(self, eng, upto):
        for sl in range(len(self.s)):
            cnt = (upto - sl + len(self.s) - 1) // len(self.s)
            if cnt > 0:
                eng.wait_ge(self.s[sl], 16 * cnt)


def _build():
    nc = bass.Bass("TRN2", target_bir_lowering=False, detect_race_conditions=False)

    dr = {}
    dr["x"] = nc.dram_tensor("x", [TOK, D], F32, kind="ExternalInput")
    dr["xh"] = nc.dram_tensor("xh", [3, D], F32, kind="ExternalInput")
    dr["hmask"] = nc.dram_tensor("hmask", [128, 1], F32, kind="ExternalInput")
    dr["cst"] = nc.dram_tensor("cst", [128, 4], F32, kind="ExternalInput")
    dr["sel"] = nc.dram_tensor("sel", [16, 3], F32, kind="ExternalInput")
    dr["selinv"] = nc.dram_tensor("selinv", [16, 3], F32, kind="ExternalInput")
    dr["wi"] = nc.dram_tensor("wi", [2064, D], F32, kind="ExternalInput")
    dr["bi"] = nc.dram_tensor("bi", [2176], F32, kind="ExternalInput")
    dr["wcdiag"] = nc.dram_tensor("wcdiag", [16, 4, 128, 128], F32, kind="ExternalInput")
    dr["bc"] = nc.dram_tensor("bc", [DI], F32, kind="ExternalInput")
    dr["wdbl"] = nc.dram_tensor("wdbl", [48, DI], F32, kind="ExternalInput")
    dr["b48"] = nc.dram_tensor("b48", [48, 1], F32, kind="ExternalInput")
    dr["wo16"] = nc.dram_tensor("wo16", [D, DS], F32, kind="ExternalInput")
    dr["gw"] = nc.dram_tensor("gw", [D, NE], F32, kind="ExternalInput")
    dr["gb"] = nc.dram_tensor("gb", [NE, 1], F32, kind="ExternalInput")
    dr["ew"] = nc.dram_tensor("ew", [NE, D, D], F32, kind="ExternalInput")
    dr["eb"] = nc.dram_tensor("eb", [NE, D], F32, kind="ExternalInput")
    dr["idin"] = nc.dram_tensor("idin", [128, 128], F32, kind="ExternalInput")
    dr["out"] = nc.dram_tensor("out", [TOK, D], F32, kind="ExternalOutput")
    stg_kind = "ExternalOutput" if _DBG else "Internal"
    dr["xn_dram"] = nc.dram_tensor("xn_dram", [8, 128, TOK], BF16, kind=stg_kind)
    dr["dbl_dram"] = nc.dram_tensor("dbl_dram", [48, TOK], F32, kind=stg_kind)
    dr["z16_dram"] = nc.dram_tensor("z16_dram", [16, TOK], F32, kind=stg_kind)
    dr["x16_dram"] = nc.dram_tensor("x16_dram", [16, TOK], F32, kind=stg_kind)
    dr["gate_dram"] = nc.dram_tensor("gate_dram", [8, TOK], F32, kind=stg_kind)
    dr["ag_in"] = nc.dram_tensor("ag_in", [16, 2], F32)
    dr["ag_out"] = nc.dram_tensor("ag_out", [4, 16, 2], F32)
    if _DBG:
        dr["d_wtm"] = nc.dram_tensor("d_wtm", [128, 256], F32, kind="ExternalOutput")
        dr["d_wfm"] = nc.dram_tensor("d_wfm", [8, TOK], BF16, kind="ExternalOutput")
        dr["d_hz"] = nc.dram_tensor("d_hz", [16, TOK], BF16, kind="ExternalOutput")
        dr["d_hloc"] = nc.dram_tensor("d_hloc", [16, TOK], F32, kind="ExternalOutput")
        dr["d_cf"] = nc.dram_tensor("d_cf", [16, 8], F32, kind="ExternalOutput")
        dr["d_gath"] = nc.dram_tensor("d_gath", [16, 8], F32, kind="ExternalOutput")
        dr["d_lnb"] = nc.dram_tensor("d_lnb", [128, 16], F32, kind="ExternalOutput")

    pend = []

    class Region:
        def __init__(self, base):
            self.off = base

        def take(self, name, shape, dtype):
            per = int(np.prod(shape[1:])) * mybir.dt.size(dtype)
            per = (per + 31) & ~31
            pend.append((name, shape, dtype, self.off))
            self.off += per

    P = Region(0)
    for nm, sh, dt in [
        ("idbf", [128, 128], BF16), ("idf32", [128, 128], F32),
        ("bi_sb", [128, 17], F32), ("bc_sb", [128, 16], F32),
        ("b48_sb", [48, 1], F32), ("gb_sb", [8, 1], F32),
        ("hm_sb", [128, 1], F32), ("cst_sb", [128, 4], F32),
        ("sel_sb", [16, 3], F32),
        ("seli_sb", [16, 3], F32), ("eb_sb", [8, D], BF16),
        ("gw_sb", [128, 8, 8], BF16), ("wo16m", [16, D], BF16),
        ("wfm", [8, TOK], BF16), ("wtm", [128, 256], F32),
        ("hz_bf", [16, TOK], BF16), ("haloxz", [128, 48], BF16),
        ("xnh_fm", [128, 8, 3], BF16), ("tmp48", [128, 48], F32),
    ]:
        P.take(nm, sh, dt)
    pbase = P.off

    A = Region(pbase)
    for nm, sh, dt in [
        ("wi_lhsT", [128, 8, 2064], BF16), ("diag_sb", [128, 16, 4, 128], BF16),
        ("wdbl_lhsT", [128, 16, 48], BF16),
        ("wstage0", [128, D], BF16), ("wstage1", [128, D], BF16),
        ("wdblst", [48, DI], BF16), ("wo16st", [128, 128], BF16),
        ("xbuf0", [128, G1 * D], F32), ("xbuf1", [128, G1 * D], F32),
        ("lnbuf", [128, 16], F32), ("sqdump", [128, D], F32),
        ("xn_tm0", [128, D], BF16), ("xn_tm1", [128, D], BF16),
        ("xnf0", [128, 8, T1], BF16), ("xnf1", [128, 8, T1], BF16),
        ("xn1b0", [128, 8, T1], BF16), ("xn1b1", [128, 8, T1], BF16),
        ("cbuf0", [128, 515], BF16), ("cbuf1", [128, 515], BF16),
        ("xcb0", [128, T1], BF16), ("xcb1", [128, T1], BF16),
        ("dstg0", [48, T1], F32), ("dstg1", [48, T1], F32),
        ("gstg0", [8, T1], F32), ("gstg1", [8, T1], F32),
        ("zstg0", [16, T1], F32), ("zstg1", [16, T1], F32),
        ("xstg0", [16, T1], F32), ("xstg1", [16, T1], F32),
    ]:
        A.take(nm, sh, dt)

    B = Region(pbase)
    blist = [
        ("xn2b0", [128, 8, T2], BF16),
        ("ebuf0", [128, 8, D], BF16), ("ebuf1", [128, 8, D], BF16),
        ("x2r0", [128, D], F32), ("x2r1", [128, D], F32), ("x2r2", [128, D], F32),
        ("acc", [128, 8, D], F32),
        ("zeros16", [16, CH], F32),
        ("h_loc", [16, TOK], F32), ("cumP", [16, TOK], F32),
        ("gfull", [8, TOK], F32),
        ("glog", [128, 8], F32), ("esb", [128, 8], F32), ("psb", [128, 8], F32),
        ("msk", [128, 8], F32), ("sft", [128, 4], F32), ("sftn", [128, 1], F32),
        ("gmx", [128, 8], F32),
        ("smry", [16, 2], F32), ("gath", [16, 8], F32), ("cfold", [16, 8], F32),
    ]
    for nm in ["inA", "inB", "inD", "inX", "tS1", "tS2", "tD", "tV1", "Abf", "Bbf", "zb"]:
        for p in range(2):
            blist.append((f"{nm}{p}", [16, CH], F32))
    for nm, sh, dt in blist:
        B.take(nm, sh, dt)

    total = max(A.off, B.off)
    assert total <= 196 * KB, f"SBUF budget blown: {total / KB:.1f} KB"

    sb = {}
    with nc.sbuf_tensor("arena", [128, total], U8) as arena:
        base_addr = nc.lookup_mloc(arena).addr
        for name, shape, dtype, off in pend:
            sb[name] = nc.alloc_sbuf_tensor_at(name, shape, dtype, offset=base_addr + off)
        _prog(nc, sb, dr)
    return nc


def _prog(nc, sb, dr):
    PH = _PHASES
    from contextlib import ExitStack
    with ExitStack() as st:
        P01 = st.enter_context(nc.psum_tensor("P01", [128, 1024], F32))
        P23 = st.enter_context(nc.psum_tensor("P23", [128, 1024], F32))
        P4 = st.enter_context(nc.psum_tensor("P4", [128, 512], F32))
        P5 = st.enter_context(nc.psum_tensor("P5", [128, 512], F32))
        P6 = st.enter_context(nc.psum_tensor("P6", [128, 512], F32))
        P7 = st.enter_context(nc.psum_tensor("P7", [128, 512], F32))

        def sem(name):
            return st.enter_context(nc.semaphore(name))

        QX = QSem(sem, "qx", 2)
        QWLD = QSem(sem, "qw", 3)
        QXND = QSem(sem, "qn", 2)
        QX1 = QSem(sem, "q1", 2)
        QST = QSem(sem, "qs", 2)
        QSS = QSem(sem, "qp", 3)
        QEB = QSem(sem, "qe", 2)
        QX2 = QSem(sem, "q2", 3)
        QXN2 = QSem(sem, "qq", 2)
        QOUT = QSem(sem, "qo", 2)

        c_lns, c_tr, c_evV, c_evS = sem("c_lns"), sem("c_tr"), sem("c_evV"), sem("c_evS")
        c_mmxz, c_evxz, c_cv, c_sil = sem("c_mmxz"), sem("c_evxz"), sem("c_cv"), sem("c_sil")
        c_dblm, c_evdbl, c_gate, c_evgate = sem("c_dblm"), sem("c_evdbl"), sem("c_gate"), sem("c_evgate")
        c_hin, c_sp, c_pp, c_scan = sem("c_hin"), sem("c_sp"), sem("c_pp"), sem("c_scan")
        c_smy, c_cc, c_gvv, c_gtf = sem("c_smy"), sem("c_cc"), sem("c_gvv"), sem("c_gtf")
        c_gse, c_gex = sem("c_gse"), sem("c_gex")
        c_gtb, c_wfm, c_hzc, c_mmG, c_mmE = sem("c_gtb"), sem("c_wfm"), sem("c_hzc"), sem("c_mmG"), sem("c_mmE")
        c_stt, c_stt0, c_mmM, c_madd = sem("c_stt"), sem("c_stt0"), sem("c_mmM"), sem("c_madd")
        block = st.enter_context(nc.Block())
        pxz = [P4, P5]
        pcv = [P6, P7]
        pdbl = P01[0:48, 0:512]
        pgate = P01[0:8, 512:1024]
        p23bf = P23.bitcast(BF16)
        ptr = [p23bf[:, 0:128], p23bf[:, 1024:1152]]
        pe = [P01, P23]
        pm = [P4, P5]
        pgf = P6[:, 0:8]
        pgb = P7[0:8, 0:128]

        TR = [("xn", 0, k) for k in range(8)]
        for i in range(1, NG):
            TR += [("xn", i, k) for k in range(8)]
            if i == 2:
                TR += [("wo16", 0, k) for k in range(8)]
            elif i == 3:
                TR += [("wdbl", 0, k) for k in range(8)]
            elif i == 4:
                TR += [("wdbl", 0, k) for k in range(8, 16)]
            elif 5 <= i <= 21:
                TR += [("wi", i - 5, k) for k in range(8)]
        nV = nS = 0
        plan = []
        last_tr_of_group = {}
        last_tr_of_wichunk = {}
        last_evV_of_group = {}
        for n, (kind, idx, k) in enumerate(TR):
            if kind == "xn":
                nV += 1
                ev = ("V", nV)
                last_tr_of_group[idx] = n + 1
                last_evV_of_group[idx] = nV
            else:
                nS += 1
                ev = ("S", nS)
                if kind == "wi":
                    last_tr_of_wichunk[idx] = n + 1
            plan.append((kind, idx, k, ev))
        N_EVS = nS

        def gtile(i):
            return (i - 1) // G1

        (W_IDBF, W_IDF, W_BI, W_BC, W_B48, W_GB, W_HM, W_CST, W_SEL,
         W_SELI, W_EB, W_GW, W_WO, W_WDBL, W_DGA, W_DGB) = range(16)

        def w_wi(r):
            return 16 + r

        W_TOT = w_wi(16) + 1

        def evxz_cnt(t, m):
            return (m + 1) if t < 0 else (16 + 17 * t + m + 1)

        def st_idx(t, j):
            return 4 * t + j

        SS_GF = 0

        def ss_chunk(c, j):
            return 1 + 4 * c + j

        SS_AGIN = 1 + 4 * NCH
        SS_GATH = 2 + 4 * NCH

        def ss_z(c):
            return 3 + 4 * NCH + c

        @block.sync
        def _(sy):
            def ldx(t):
                if t < 0:
                    QX.issue(sy, lambda: sy.dma_start(sb["xbuf0"][0:3, 0:D], dr["xh"][:, :]))
                    return
                slot = sb["xbuf1"] if t % 2 == 0 else sb["xbuf0"]
                if t == 1:
                    sy.wait_ge(c_lns, 1)
                elif t >= 2:
                    sy.wait_ge(c_lns, 4 * (t - 2) + 5)
                src = dr["x"][t * T1:(t + 1) * T1, :].rearrange("(g p) f -> p g f", p=128)
                QX.issue(sy, lambda: sy.dma_start(
                    slot[:, :].rearrange("p (g f) -> p g f", g=G1), src))

            ldx(-1)
            ldx(0)
            ldx(1)
            for t in range(NT1):
                sy.wait_ge(c_evV, last_evV_of_group[1 + 4 * t + 3])
                xnf = sb["xnf0"] if t % 2 == 0 else sb["xnf1"]
                dst = dr["xn_dram"][:, :, t * T1:(t + 1) * T1].rearrange("k p c -> p k c")
                QXND.issue(sy, lambda: sy.dma_start(dst, xnf[:, :, :]))
                if t + 2 < NT1:
                    ldx(t + 2)

            if PH == "0":
                QXND.wait_all(sy, NT1)
                if _DBG:
                    QOUT.issue(sy, lambda: sy.dma_start(dr["d_lnb"][:, :], sb["lnbuf"][:, :]))
                    QOUT.wait_all(sy, 1)
                return

            def stores(t):
                par = t % 2
                sy.wait_ge(c_sil, 16 * t + 1)
                QST.issue(sy, lambda: sy.dma_start(
                    dr["x16_dram"][:, t * T1:(t + 1) * T1], sb[f"xstg{par}"][:, :]))
                sy.wait_ge(c_evxz, evxz_cnt(t, 16))
                QST.issue(sy, lambda: sy.dma_start(
                    dr["z16_dram"][:, t * T1:(t + 1) * T1], sb[f"zstg{par}"][:, :]))
                sy.wait_ge(c_evdbl, t + 1)
                QST.issue(sy, lambda: sy.dma_start(
                    dr["dbl_dram"][:, t * T1:(t + 1) * T1], sb[f"dstg{par}"][:, :]))
                sy.wait_ge(c_evgate, t + 1)
                QST.issue(sy, lambda: sy.dma_start(
                    dr["gate_dram"][:, t * T1:(t + 1) * T1], sb[f"gstg{par}"][:, :]))

            if PH in _HALO_MODES:
                QXND.wait_all(sy, NT1)
                return
            NTX = _NT1X
            for t in range(NTX):
                if t >= 2:
                    stores(t - 2)
                QXND.wait_done(sy, t)
                if t >= 2:
                    sy.wait_ge(c_gate, t - 1)
                xn1 = sb["xn1b0"] if t % 2 == 0 else sb["xn1b1"]
                src = dr["xn_dram"][:, :, t * T1:(t + 1) * T1].rearrange("k p c -> p k c")
                QX1.issue(sy, lambda: sy.dma_start(xn1[:, :, :], src))
            if NTX >= 2:
                stores(NTX - 2)
            if NTX >= 1:
                stores(NTX - 1)
            if NTX < NT1:
                QST.wait_all(sy, 4 * NTX)
                return
            if PH == "01":
                QST.wait_all(sy, 4 * NT1)
                return
            sy.wait_ge(c_evgate, NT1)
            sy.wait_ge(c_gate, NT1)
            QST.wait_all(sy, 4 * NT1)
            QSS.issue(sy, lambda: sy.dma_start(sb["gfull"][:, :], dr["gate_dram"][:, :]))
            if PH == "full":
                src0 = dr["xn_dram"][:, :, 0:T2].rearrange("k p c -> p k c")
                QXN2.issue(sy, lambda: sy.dma_start(sb["xn2b0"][:, :, :], src0))
                for g in range(3):
                    QX2.issue(sy, lambda: sy.dma_start(
                        sb[f"x2r{g % 3}"][:, :], dr["x"][g * 128:(g + 1) * 128, :]))
            for c in range(NCH):
                p = c % 2
                if c >= 2:
                    sy.wait_ge(c_scan, 2 * (c - 2) + 2)
                QSS.issue(sy, lambda: sy.dma_start(
                    sb[f"inA{p}"][:, :], dr["dbl_dram"][0:16, c * CH:(c + 1) * CH]))
                QSS.issue(sy, lambda: sy.dma_start(
                    sb[f"inB{p}"][:, :], dr["dbl_dram"][16:32, c * CH:(c + 1) * CH]))
                QSS.issue(sy, lambda: sy.dma_start(
                    sb[f"inD{p}"][:, :], dr["dbl_dram"][32:48, c * CH:(c + 1) * CH]))
                QSS.issue(sy, lambda: sy.dma_start(
                    sb[f"inX{p}"][:, :], dr["x16_dram"][:, c * CH:(c + 1) * CH]))
            sy.wait_ge(c_smy, 2)
            QSS.issue(sy, lambda: sy.dma_start(dr["ag_in"][:, :], sb["smry"][:, :]))
            if PH == "full":
                for g in range(3, G2):
                    sy.wait_ge(c_stt0, g - 2)
                    QX2.issue(sy, lambda: sy.dma_start(
                        sb[f"x2r{g % 3}"][:, :], dr["x"][g * 128:(g + 1) * 128, :]))
            sy.wait_ge(c_smy, 3)
            QSS.issue(sy, lambda: sy.dma_start(
                sb["gath"][:, :].rearrange("p (r c) -> p r c", r=4),
                dr["ag_out"][:, :, :].rearrange("r p c -> p r c")))
            for c in range(NCH):
                if c >= 2:
                    sy.wait_ge(c_hzc, 2 * (c - 2) + 2)
                QSS.issue(sy, lambda: sy.dma_start(
                    sb[f"zb{c % 2}"][:, :], dr["z16_dram"][:, c * CH:(c + 1) * CH]))
            if PH == "01S":
                QSS.wait_all(sy, 3 + 5 * NCH)
                sy.wait_ge(c_hzc, 2 * NCH)
                return
            for t in range(1, NT2):
                sy.wait_ge(c_mmE, 8 * t)
                src2 = dr["xn_dram"][:, :, t * T2:(t + 1) * T2].rearrange("k p c -> p k c")
                QXN2.issue(sy, lambda: sy.dma_start(sb["xn2b0"][:, :, :], src2))
                sy.wait_ge(c_madd, 16 * (t - 1) + 16)
                dst1 = dr["out"][(t - 1) * T2:t * T2, :].rearrange("(g p) f -> p g f", p=128)
                QOUT.issue(sy, lambda: sy.dma_start(dst1, sb["acc"][:, :, :]))
                for g in range(G2):
                    gx = 8 * t + g
                    sy.wait_ge(c_stt0, max(0, gx - 2))
                    QX2.issue(sy, lambda: sy.dma_start(
                        sb[f"x2r{gx % 3}"][:, :],
                        dr["x"][t * T2 + g * 128:t * T2 + (g + 1) * 128, :]))
            sy.wait_ge(c_madd, 16 * (NT2 - 1) + 16)
            dst2 = dr["out"][(NT2 - 1) * T2:NT2 * T2, :].rearrange("(g p) f -> p g f", p=128)
            QOUT.issue(sy, lambda: sy.dma_start(dst2, sb["acc"][:, :, :]))
            QOUT.wait_all(sy, NT2)
            if _DBG:
                for nmA, nmB in [("d_wtm", "wtm"), ("d_wfm", "wfm"), ("d_hz", "hz_bf"),
                                 ("d_hloc", "h_loc"), ("d_cf", "cfold"), ("d_gath", "gath")]:
                    QOUT.issue(sy, lambda nmA=nmA, nmB=nmB: sy.dma_start(
                        dr[nmA][:, :], sb[nmB][:, :]))
                QOUT.wait_all(sy, NT2 + 6)

        @block.gpsimd
        def _(g):
            def ld(mk):
                QWLD.issue(g, mk)

            ld(lambda: g.dma_start(sb["idbf"][:, :], dr["idin"][:, :]))
            ld(lambda: g.dma_start(sb["idf32"][:, :], dr["idin"][:, :]))
            ld(lambda: g.dma_start(sb["bi_sb"][:, :], dr["bi"][:].rearrange("(m p) -> p m", p=128),
                                   allow_slow_non_contiguous=True))
            ld(lambda: g.dma_start(sb["bc_sb"][:, :], dr["bc"][:].rearrange("(m p) -> p m", p=128),
                                   allow_slow_non_contiguous=True))
            ld(lambda: g.dma_start(sb["b48_sb"][:, :], dr["b48"][:, :]))
            ld(lambda: g.dma_start(sb["gb_sb"][:, :], dr["gb"][:, :]))
            ld(lambda: g.dma_start(sb["hm_sb"][:, :], dr["hmask"][:, :]))
            ld(lambda: g.dma_start(sb["cst_sb"][:, :], dr["cst"][:, :]))
            ld(lambda: g.dma_start(sb["sel_sb"][:, :], dr["sel"][:, :]))
            ld(lambda: g.dma_start(sb["seli_sb"][:, :], dr["selinv"][:, :]))
            ld(lambda: g.dma_start(sb["eb_sb"][:, :], dr["eb"][:, :]))
            ld(lambda: g.dma_start(sb["gw_sb"][:, :, :],
                                   dr["gw"][:, :].rearrange("(k p) e -> p k e", p=128)))
            ld(lambda: g.dma_start(sb["wo16st"][:, :].rearrange("p (k s) -> p k s", k=8),
                                   dr["wo16"][:, :].rearrange("(k p) s -> p k s", p=128)))
            ld(lambda: g.dma_start(sb["wdblst"][:, :], dr["wdbl"][:, :]))
            ld(lambda: g.dma_start(sb["diag_sb"][:, 0:8, :, :],
                                   dr["wcdiag"][0:8, :, :, :].rearrange("m t i j -> i m t j")))
            ld(lambda: g.dma_start(sb["diag_sb"][:, 8:16, :, :],
                                   dr["wcdiag"][8:16, :, :, :].rearrange("m t i j -> i m t j")))
            for r in range(17):
                if r >= 2:
                    g.wait_ge(c_tr, last_tr_of_wichunk[r - 2])
                stg = sb["wstage0"] if r % 2 == 0 else sb["wstage1"]
                if r < 16:
                    ld(lambda: g.dma_start(stg[:, :], dr["wi"][r * 128:(r + 1) * 128, :]))
                else:
                    ld(lambda: g.dma_start(stg[0:16, :], dr["wi"][2048:2064, :]))
            if PH in ("0", "01") or PH in _HALO_MODES or _NT1X < NT1:
                return
            g.wait_ge(c_gate, NT1)
            g.wait_ge(c_evgate, NT1)
            QXND.wait_all(g, NT1)
            QST.wait_all(g, 4 * NT1)
            g.wait_ge(c_evxz, evxz_cnt(NT1 - 1, 16))
            g.wait_ge(c_sil, 16 * NT1)

            def eb_load(idx):
                if idx >= 2:
                    g.wait_ge(c_mmE, idx - 1)
                eb = sb["ebuf0"] if idx % 2 == 0 else sb["ebuf1"]
                e = idx % NE
                src = dr["ew"][e, :, :].rearrange("(k p) n -> p k n", p=128)
                QEB.issue(g, lambda: g.dma_start(eb[:, :, :], src))

            def chunk_pool(c):
                p = c % 2
                if c >= 2:
                    g.wait_ge(c_sp, 5 * (c - 2) + 5)
                QSS.wait_done(g, ss_chunk(c, 2))
                g.tensor_single_scalar(sb[f"tV1{p}"][:, :], sb[f"inD{p}"][:, :], 0.0, ALU.max)
                g.drain().then_inc(c_pp, 1)
                g.wait_ge(c_sp, 5 * c + 3)
                g.tensor_add(sb[f"tD{p}"][:, :], sb[f"tS2{p}"][:, :], sb[f"tV1{p}"][:, :])
                g.drain().then_inc(c_pp, 1)
                g.wait_ge(c_sp, 5 * c + 4)
                g.tensor_mul(sb[f"tV1{p}"][:, :], sb[f"tS1{p}"][:, :], sb[f"tD{p}"][:, :])
                g.drain().then_inc(c_pp, 1)
                QSS.wait_done(g, ss_chunk(c, 1))
                g.tensor_mul(sb[f"tS2{p}"][:, :], sb[f"inB{p}"][:, :], sb[f"tD{p}"][:, :])
                g.drain().then_inc(c_pp, 1)
                QSS.wait_done(g, ss_chunk(c, 3))
                g.tensor_mul(sb[f"Bbf{p}"][:, :], sb[f"tS2{p}"][:, :], sb[f"inX{p}"][:, :])
                g.drain().then_inc(c_pp, 1)

            if PH == "full":
                eb_load(0)
                eb_load(1)
            for c in range(NCH):
                chunk_pool(c)
            if PH == "01S":
                QSS.wait_done(g, SS_AGIN)
                g.collective_compute(
                    "AllGather", ALU.bypass,
                    replica_groups=[[0, 1, 2, 3], [4, 5, 6, 7]],
                    ins=[dr["ag_in"][:, :].opt()],
                    outs=[dr["ag_out"][:, :, :].opt()],
                ).then_inc(c_cc, 1)
                return
            for idx in range(2, NT2 * NE):
                eb_load(idx)
                if idx == 4:
                    QSS.wait_done(g, SS_AGIN)
                    g.collective_compute(
                        "AllGather", ALU.bypass,
                        replica_groups=[[0, 1, 2, 3], [4, 5, 6, 7]],
                        ins=[dr["ag_in"][:, :].opt()],
                        outs=[dr["ag_out"][:, :, :].opt()],
                    ).then_inc(c_cc, 1)

        def _dblmm(t_, t, m):
            t_.wait_ge(c_sil, 16 * t + m + 1)
            if m == 0:
                t_.wait_ge(c_evdbl, t)
            xc = sb["xcb0"] if m % 2 == 0 else sb["xcb1"]
            t_.matmul(pdbl[:, :], sb["wdbl_lhsT"][:, m, :], xc[:, :],
                      start=(m == 0), stop=(m == 15))
            t_.drain().then_inc(c_dblm, 1)

        @block.tensor
        def _(t_):
            for n, (kind, idx, k, ev) in enumerate(plan):
                if n >= 2:
                    pev = plan[n - 2][3]
                    t_.wait_ge(c_evV if pev[0] == "V" else c_evS, pev[1])
                dst = ptr[n % 2]
                if kind == "xn":
                    if k == 0:
                        t_.wait_ge(c_lns, idx + 1)
                    src_t = sb["xn_tm0"] if idx % 2 == 0 else sb["xn_tm1"]
                    rows = 3 if idx == 0 else 128
                    t_.transpose(dst[0:128, 0:rows], src_t[0:rows, k * 128:(k + 1) * 128],
                                 sb["idbf"][0:rows, 0:rows]).then_inc(c_tr, 1)
                elif kind == "wi":
                    if k == 0:
                        QWLD.wait_done(t_, w_wi(idx))
                    stg = sb["wstage0"] if idx % 2 == 0 else sb["wstage1"]
                    rows = 128 if idx < 16 else 16
                    t_.transpose(dst[0:128, 0:rows], stg[0:rows, k * 128:(k + 1) * 128],
                                 sb["idbf"][0:rows, 0:rows]).then_inc(c_tr, 1)
                elif kind == "wdbl":
                    if k in (0, 8):
                        QWLD.wait_done(t_, W_WDBL)
                    t_.transpose(dst[0:128, 0:48], sb["wdblst"][0:48, k * 128:(k + 1) * 128],
                                 sb["idbf"][0:48, 0:48]).then_inc(c_tr, 1)
                else:
                    if k == 0:
                        QWLD.wait_done(t_, W_WO)
                    t_.transpose(dst[0:16, 0:128], sb["wo16st"][:, k * 16:(k + 1) * 16],
                                 sb["idbf"][:, :]).then_inc(c_tr, 1)

            if PH == "0":
                return
            QWLD.wait_all(t_, W_TOT)
            t_.wait_ge(c_evS, N_EVS)
            t_.wait_ge(c_evV, 8 * NG)
            for m in range(16):
                if m >= 2 and PH != "Hm":
                    t_.wait_ge(c_evxz, m - 1)
                ps = pxz[m % 2]
                for k in range(8):
                    t_.matmul(ps[:, 0:3], sb["wi_lhsT"][:, k, m * 128:(m + 1) * 128],
                              sb["xnh_fm"][:, k, :], start=(k == 0), stop=(k == 7))
                t_.drain().then_inc(c_mmxz, 1)

            if PH in _HALO_MODES:
                return
            for t in range(_NT1X):
                QX1.wait_done(t_, t)
                xn1 = sb["xn1b0"] if t % 2 == 0 else sb["xn1b1"]
                for m in range(17):
                    u = 17 * t + m
                    t_.wait_ge(c_evxz, evxz_cnt(t, m) - 2)
                    ps = pxz[(16 + u) % 2]
                    rows = 128 if m < 16 else 16
                    for k in range(8):
                        t_.matmul(ps[0:rows, :], sb["wi_lhsT"][:, k, m * 128:m * 128 + rows],
                                  xn1[:, k, :], start=(k == 0), stop=(k == 7))
                    t_.drain().then_inc(c_mmxz, 1)
                    if m >= 1:
                        mm = m - 1
                        uc = 16 * t + mm
                        if t == 0 and mm < 2:
                            QWLD.wait_done(t_, W_DGA)
                            QWLD.wait_done(t_, W_DGB)
                        t_.wait_ge(c_sil, max(0, uc - 1))
                        t_.wait_ge(c_evxz, evxz_cnt(t, mm))
                        t_.wait_ge(c_hin, uc + 1)
                        pc = pcv[uc % 2]
                        cb = sb["cbuf0"] if mm % 2 == 0 else sb["cbuf1"]
                        for tap in range(4):
                            t_.matmul(pc[:, :], sb["diag_sb"][:, mm, tap, :],
                                      cb[:, tap:tap + 512], start=(tap == 0), stop=(tap == 3))
                        t_.drain().then_inc(c_cv, 1)
                        if mm >= 1:
                            _dblmm(t_, t, mm - 1)
                _dblmm(t_, t, 15)
                t_.wait_ge(c_evgate, t)
                for k in range(8):
                    t_.matmul(pgate[:, :], sb["gw_sb"][:, k, :], xn1[:, k, :],
                              start=(k == 0), stop=(k == 7))
                t_.drain().then_inc(c_gate, 1)

            if PH in ("01",) or _NT1X < NT1:
                return
            t_.wait_ge(c_evdbl, NT1)
            t_.wait_ge(c_evgate, NT1)
            t_.wait_ge(c_evxz, evxz_cnt(NT1 - 1, 16))
            t_.wait_ge(c_sil, 16 * NT1)
            QSS.wait_done(t_, SS_GF)

            for gg in range(32):
                if gg >= 1:
                    t_.wait_ge(c_gse, gg)
                t_.transpose(pgf[:, :], sb["gfull"][:, gg * 128:(gg + 1) * 128],
                             sb["idf32"][0:8, 0:8])
                t_.drain().then_inc(c_gtf, 1)
                t_.wait_ge(c_gvv, 8 * gg + 8)
                if gg >= 1:
                    t_.wait_ge(c_wfm, gg)
                t_.transpose(pgb[:, :], sb["wtm"][:, 8 * gg:8 * (gg + 1)],
                             sb["idf32"][:, :])
                t_.drain().then_inc(c_gtb, 1)
            if PH == "01S":
                return
            for t in range(NT2):
                QXN2.wait_done(t_, t)
                xn2 = sb["xn2b0"]
                for e in range(NE):
                    QEB.wait_done(t_, 8 * t + e)
                    eb = sb["ebuf0"] if (8 * t + e) % 2 == 0 else sb["ebuf1"]
                    for g_ in range(G2):
                        geidx = t * 64 + e * 8 + g_
                        pp = pe[geidx % 2]
                        for nh in range(2):
                            if geidx >= 2:
                                t_.wait_ge(c_stt, 2 * (geidx - 2) + nh + 1)
                            for k in range(8):
                                t_.matmul(pp[:, nh * 512:(nh + 1) * 512],
                                          xn2[:, k, g_ * 128:(g_ + 1) * 128],
                                          eb[:, k, nh * 512:(nh + 1) * 512],
                                          start=(k == 0), stop=(k == 7))
                            t_.drain().then_inc(c_mmG, 1)
                    t_.drain().then_inc(c_mmE, 1)
                if t == 0:
                    t_.wait_ge(c_hzc, 2 * NCH)
                    t_.wait_ge(c_wfm, 32)
                for g_ in range(G2):
                    for nh in range(2):
                        mcnt = 16 * t + 2 * g_ + nh + 1
                        t_.wait_ge(c_madd, max(0, mcnt - 2))
                        pmx = pm[(2 * g_ + nh) % 2]
                        c0 = t * T2 + g_ * 128
                        t_.matmul(pmx[:, :], sb["hz_bf"][:, c0:c0 + 128],
                                  sb["wo16m"][:, nh * 512:(nh + 1) * 512],
                                  start=True, stop=False)
                        t_.matmul(pmx[:, :], sb["wfm"][:, c0:c0 + 128],
                                  sb["eb_sb"][:, nh * 512:(nh + 1) * 512],
                                  start=False, stop=True)
                        t_.drain().then_inc(c_mmM, 1)

        @block.vector
        def _(v):
            v = _DrainingEngine(v)

            def vspace(n=1):
                for _i in range(n):
                    v.tensor_copy(sb["sft"][:, 3:4], sb["sft"][:, 0:1])

            for n, (kind, idx, k, ev) in enumerate(plan):
                if ev[0] != "V":
                    continue
                i = idx
                rows = 3 if i == 0 else 128
                if i >= 1:
                    t = gtile(i)
                    g = (i - 1) % G1
                    if g == 0 and k == 0 and t >= 2:
                        QXND.wait_done(v, t - 2)
                    dstb = sb["xnf0"] if t % 2 == 0 else sb["xnf1"]
                    dstp = dstb[:, k, g * 128:g * 128 + rows]
                else:
                    dstp = sb["xnh_fm"][:, k, :]
                v.wait_ge(c_tr, n + 1)
                v.tensor_copy(dstp, ptr[n % 2][0:128, 0:rows]).then_inc(c_evV, 1)

            if PH in ("0", "01") or PH in _HALO_MODES or _NT1X < NT1:
                return
            v.wait_ge(c_evgate, NT1)
            v.wait_ge(c_gate, NT1)
            QST.wait_all(v, 4 * NT1)
            v.memset(sb["zeros16"][:, :], 0.0)

            def scan_chunk(c):
                p = c % 2
                v.wait_ge(c_sp, 5 * c + 5)
                v.wait_ge(c_pp, 5 * c + 5)
                vspace()
                ini_h = 0.0 if c == 0 else sb["h_loc"][:, c * CH - 1:c * CH]
                ini_p = 1.0 if c == 0 else sb["cumP"][:, c * CH - 1:c * CH]
                v.tensor_tensor_scan(sb["h_loc"][:, c * CH:(c + 1) * CH], sb[f"Abf{p}"][:, :],
                                     sb[f"Bbf{p}"][:, :], ini_h, ALU.mult, ALU.add).then_inc(c_scan, 1)
                v.tensor_tensor_scan(sb["cumP"][:, c * CH:(c + 1) * CH], sb[f"Abf{p}"][:, :],
                                     sb["zeros16"][:, :], ini_p, ALU.mult, ALU.add).then_inc(c_scan, 1)

            for gg in range(32):
                v.wait_ge(c_gex, gg + 1)
                v.reduce_sum(sb["sft"][:, 1:2], sb["esb"][:, :], axis=AX.X).then_inc(c_gvv, 1)
                v.reciprocal(sb["sft"][:, 2:3], sb["sft"][:, 1:2]).then_inc(c_gvv, 1)
                vspace()
                v.tensor_single_scalar(sb["psb"][:, :], sb["esb"][:, :], sb["sft"][:, 2:3], ALU.mult).then_inc(c_gvv, 1)
                v.tensor_single_scalar(sb["msk"][:, :], sb["psb"][:, :], 0.1, ALU.is_ge).then_inc(c_gvv, 1)
                v.tensor_mul(sb["psb"][:, :], sb["psb"][:, :], sb["msk"][:, :]).then_inc(c_gvv, 1)
                v.reduce_sum(sb["sft"][:, 1:2], sb["psb"][:, :], axis=AX.X).then_inc(c_gvv, 1)
                v.reciprocal(sb["sft"][:, 2:3], sb["sft"][:, 1:2]).then_inc(c_gvv, 1)
                vspace()
                v.tensor_single_scalar(sb["wtm"][:, 8 * gg:8 * (gg + 1)], sb["psb"][:, :],
                                       sb["sft"][:, 2:3], ALU.mult).then_inc(c_gvv, 1)
                if gg % 2 == 1:
                    scan_chunk(gg // 2)

            def stts(t, e):
                for g_ in range(G2):
                    geidx = t * 64 + e * 8 + g_
                    gx = 8 * t + g_
                    pp = pe[geidx % 2]
                    for nh in range(2):
                        v.wait_ge(c_mmG, 2 * geidx + nh + 1)
                        accsl = sb["acc"][:, g_, nh * 512:(nh + 1) * 512]
                        if e == 0:
                            QX2.wait_done(v, gx)
                            in1 = sb[f"x2r{gx % 3}"][:, nh * 512:(nh + 1) * 512]
                        else:
                            in1 = accsl
                        wcol = sb["wtm"][:, gx * 8 + e:gx * 8 + e + 1]
                        v.scalar_tensor_tensor(accsl, pp[:, nh * 512:(nh + 1) * 512],
                                               wcol, in1, ALU.mult, ALU.add).then_inc(c_stt, 1)
                        if e == 0 and nh == 1:
                            v.nop().then_inc(c_stt0, 1)

            def madds(t):
                for g_ in range(G2):
                    for nh in range(2):
                        v.wait_ge(c_mmM, 16 * t + 2 * g_ + nh + 1)
                        accsl = sb["acc"][:, g_, nh * 512:(nh + 1) * 512]
                        v.tensor_add(accsl, accsl, pm[(2 * g_ + nh) % 2][:, :]).then_inc(c_madd, 1)

            v.tensor_copy(sb["smry"][:, 0:1], sb["cumP"][:, TOK - 1:TOK]).then_inc(c_smy, 1)
            v.tensor_copy(sb["smry"][:, 1:2], sb["h_loc"][:, TOK - 1:TOK]).then_inc(c_smy, 1)
            if PH == "full":
                stts(0, 0)
                stts(0, 1)
                stts(0, 2)
                stts(0, 3)
            v.wait_ge(c_cc, 1)
            v.nop().then_inc(c_smy, 1)
            QSS.wait_done(v, SS_GATH)
            QWLD.wait_done(v, W_SEL)
            QWLD.wait_done(v, W_SELI)
            C = sb["cfold"]
            v.memset(C[:, 0:1], 0.0)
            vspace(2)
            for j in range(3):
                v.tensor_single_scalar(C[:, 2:3], sb["gath"][:, 2 * j:2 * j + 1], sb["sel_sb"][:, j:j + 1], ALU.mult)
                vspace(2)
                v.tensor_add(C[:, 2:3], C[:, 2:3], sb["seli_sb"][:, j:j + 1])
                vspace(2)
                v.tensor_single_scalar(C[:, 3:4], sb["gath"][:, 2 * j + 1:2 * j + 2], sb["sel_sb"][:, j:j + 1], ALU.mult)
                vspace(2)
                v.scalar_tensor_tensor(C[:, (j + 1) % 2:(j + 1) % 2 + 1], C[:, j % 2:j % 2 + 1],
                                       C[:, 2:3], C[:, 3:4], ALU.mult, ALU.add)
                vspace(2)
            carry = C[:, 4:5]
            v.tensor_copy(carry, C[:, 1:2])
            vspace(2)
            for c in range(NCH):
                cols = slice(c * CH, (c + 1) * CH)
                v.scalar_tensor_tensor(sb["h_loc"][:, cols], sb["cumP"][:, cols], carry,
                                       sb["h_loc"][:, cols], ALU.mult, ALU.add).then_inc(c_hzc, 1)
                QSS.wait_done(v, ss_z(c))
                v.tensor_mul(sb["hz_bf"][:, cols], sb["h_loc"][:, cols], sb[f"zb{c % 2}"][:, :]).then_inc(c_hzc, 1)
                if PH == "full":
                    if c == 3:
                        stts(0, 4)
                    elif c == 7:
                        stts(0, 5)
                    elif c == 11:
                        stts(0, 6)
                    elif c == 15:
                        stts(0, 7)
            if PH == "01S":
                return
            madds(0)
            for t in range(1, NT2):
                QOUT.wait_done(v, t - 1)
                for e in range(NE):
                    stts(t, e)
                madds(t)

        @block.scalar
        def _(s):
            s = _DrainingEngine(s)

            def ln_s(i):
                rows = 3 if i == 0 else 128
                if i == 0:
                    xg = sb["xbuf0"][0:3, 0:D]
                    QX.wait_done(s, 0)
                else:
                    t = gtile(i)
                    g = (i - 1) % G1
                    xb = sb["xbuf1"] if t % 2 == 0 else sb["xbuf0"]
                    xg = xb[:, g * D:(g + 1) * D]
                    QX.wait_done(s, t + 1)
                L = sb["lnbuf"]
                cst = sb["cst_sb"]
                o = (i % 2) * 8

                def Lc(c):
                    return L[0:rows, o + c:o + c + 1]

                s.activation(sb["sqdump"][0:rows, :], xg, AF.Identity, accum_out=Lc(0))
                s.activation(sb["sqdump"][0:rows, :], xg, AF.Square, accum_out=Lc(1))
                s.activation(Lc(2), Lc(0), AF.Identity, scale=cst[0:rows, 0:1])
                s.activation(Lc(3), Lc(1), AF.Identity, scale=cst[0:rows, 0:1],
                             bias=cst[0:rows, 1:2])
                s.activation(Lc(4), Lc(2), AF.Identity, scale=-1.0)
                s.activation(Lc(5), Lc(2), AF.Identity, scale=Lc(2))
                s.activation(Lc(6), Lc(5), AF.Identity, scale=-1.0, bias=Lc(3))
                s.activation(Lc(7), Lc(6), AF.Ln)
                s.activation(Lc(5), Lc(7), AF.Exp, scale=cst[0:rows, 2:3])
                if i >= 2:
                    s.wait_ge(c_tr, last_tr_of_group[i - 2])
                xnt = sb["xn_tm0"] if i % 2 == 0 else sb["xn_tm1"]
                s.activation(sb["sqdump"][0:rows, :], xg, AF.Identity, bias=Lc(4))
                s.activation(xnt[0:rows, :], sb["sqdump"][0:rows, :], AF.Identity,
                             scale=Lc(5)).then_inc(c_lns, 1)

            QWLD.wait_done(s, W_CST)
            ln_s(0)
            ln_s(1)
            for n, (kind, idx, k, ev) in enumerate(plan):
                if kind == "xn" and k == 0 and 1 <= idx and idx + 1 < NG:
                    ln_s(idx + 1)
                if ev[0] != "S":
                    continue
                s.wait_ge(c_tr, n + 1)
                if kind == "wi":
                    rows = 128 if idx < 16 else 16
                    s.activation(sb["wi_lhsT"][:, k, idx * 128:idx * 128 + rows],
                                 ptr[n % 2][0:128, 0:rows], AF.Copy).then_inc(c_evS, 1)
                elif kind == "wdbl":
                    s.activation(sb["wdbl_lhsT"][:, k, :], ptr[n % 2][0:128, 0:48],
                                 AF.Copy).then_inc(c_evS, 1)
                else:
                    s.activation(sb["wo16m"][:, k * 128:(k + 1) * 128], ptr[n % 2][0:16, 0:128],
                                 AF.Copy).then_inc(c_evS, 1)

            if PH in ("0", "Hm"):
                return
            QWLD.wait_done(s, W_BI)
            for m in range(16):
                s.wait_ge(c_mmxz, m + 1)
                if PH == "He":
                    s.activation(sb["tmp48"][:, 3 * m:3 * m + 3], pxz[m % 2][:, 0:3],
                                 AF.Identity).then_inc(c_evxz, 1)
                else:
                    s.activation(sb["tmp48"][:, 3 * m:3 * m + 3], pxz[m % 2][:, 0:3], AF.Identity,
                                 bias=sb["bi_sb"][:, m:m + 1]).then_inc(c_evxz, 1)
            if PH in ("He", "Heb"):
                return
            QWLD.wait_done(s, W_HM)
            s.activation(sb["haloxz"][:, :], sb["tmp48"][:, :], AF.Identity,
                         scale=sb["hm_sb"][:, 0:1])

            if PH == "H":
                return
            QWLD.wait_done(s, W_BC)
            QWLD.wait_done(s, W_B48)
            QWLD.wait_done(s, W_GB)
            for t in range(_NT1X):
                par = t % 2
                for m in range(17):
                    ps = pxz[(16 + 17 * t + m) % 2]
                    if m < 16:
                        u = 16 * t + m
                        cb = sb["cbuf0"] if m % 2 == 0 else sb["cbuf1"]
                        s.wait_ge(c_cv, max(0, u - 1))
                        s.activation(cb[:, 0:3], sb["haloxz"][:, 3 * m:3 * m + 3],
                                     AF.Identity).then_inc(c_hin, 1)
                        s.wait_ge(c_mmxz, evxz_cnt(t, m))
                        s.activation(cb[:, 3:515], ps[:, :], AF.Identity,
                                     bias=sb["bi_sb"][:, m:m + 1]).then_inc(c_evxz, 1)
                        s.activation(sb["haloxz"][:, 3 * m:3 * m + 3], cb[:, 512:515],
                                     AF.Identity)
                    else:
                        s.wait_ge(c_mmxz, evxz_cnt(t, m))
                        if t >= 2:
                            QST.wait_done(s, st_idx(t - 2, 1))
                        s.activation(sb[f"zstg{par}"][:, :], ps[0:16, :], AF.Identity,
                                     bias=sb["bi_sb"][0:16, 16:17]).then_inc(c_evxz, 1)
                    if m >= 1:
                        mm = m - 1
                        s.wait_ge(c_cv, 16 * t + mm + 1)
                        s.wait_ge(c_dblm, max(0, 16 * t + mm - 1))
                        pc = pcv[(16 * t + mm) % 2]
                        xc = sb["xcb0"] if mm % 2 == 0 else sb["xcb1"]
                        ins = s.activation(xc[:, :], pc[:, :], AF.Silu, bias=sb["bc_sb"][:, mm:mm + 1])
                        if mm == 0:
                            if t >= 2:
                                QST.wait_done(s, st_idx(t - 2, 0))
                            s.activation(sb[f"xstg{par}"][:, :], pc[0:16, :], AF.Silu,
                                         bias=sb["bc_sb"][0:16, 0:1]).then_inc(c_sil, 1)
                        else:
                            ins.then_inc(c_sil, 1)
                s.wait_ge(c_dblm, 16 * t + 16)
                if t >= 2:
                    QST.wait_done(s, st_idx(t - 2, 2))
                s.activation(sb[f"dstg{par}"][:, :], pdbl[:, :], AF.Identity,
                             bias=sb["b48_sb"][:, 0:1]).then_inc(c_evdbl, 1)
                s.wait_ge(c_gate, t + 1)
                if t >= 2:
                    QST.wait_done(s, st_idx(t - 2, 3))
                s.activation(sb[f"gstg{par}"][:, :], pgate[:, :], AF.Identity,
                             bias=sb["gb_sb"][:, 0:1]).then_inc(c_evgate, 1)

            if PH == "01" or _NT1X < NT1:
                return

            def chunk_acts(c):
                p = c % 2
                if c >= 2:
                    s.wait_ge(c_pp, 5 * (c - 2) + 5)
                QSS.wait_done(s, ss_chunk(c, 2))
                s.activation(sb[f"tS2{p}"][:, :], sb[f"inD{p}"][:, :], AF.Abs).then_inc(c_sp, 1)
                s.activation(sb[f"tS2{p}"][:, :], sb[f"tS2{p}"][:, :], AF.Exp, scale=-1.0).then_inc(c_sp, 1)
                s.activation(sb[f"tS2{p}"][:, :], sb[f"tS2{p}"][:, :], AF.Ln, bias=1.0).then_inc(c_sp, 1)
                QSS.wait_done(s, ss_chunk(c, 0))
                s.activation(sb[f"tS1{p}"][:, :], sb[f"inA{p}"][:, :], AF.Abs).then_inc(c_sp, 1)
                s.wait_ge(c_pp, 5 * c + 3)
                s.activation(sb[f"Abf{p}"][:, :], sb[f"tV1{p}"][:, :], AF.Exp, scale=-1.0).then_inc(c_sp, 1)

            QST.wait_all(s, 4 * NT1)
            s.activation(sb["sftn"][:, 0:1], sb["cst_sb"][:, 0:1], AF.Identity, scale=0.0)
            for gg in range(32):
                if gg >= 2:
                    s.wait_ge(c_gtb, gg - 1)
                    s.activation(sb["wfm"][0:8, 128 * (gg - 2):128 * (gg - 1)], pgb[:, :],
                                 AF.Copy).then_inc(c_wfm, 1)
                s.wait_ge(c_gtf, gg + 1)
                s.activation(sb["glog"][:, :], pgf[:, :], AF.Identity).then_inc(c_gse, 1)
                s.activation(sb["gmx"][:, :], sb["glog"][:, :], AF.Identity,
                             scale=sb["cst_sb"][:, 3:4], accum_out=sb["sftn"][:, 0:1])
                s.activation(sb["esb"][:, :], sb["glog"][:, :], AF.Exp,
                             bias=sb["sftn"][:, 0:1]).then_inc(c_gex, 1)
                if gg % 2 == 1:
                    chunk_acts(gg // 2)
            s.wait_ge(c_gtb, 31)
            s.activation(sb["wfm"][0:8, 128 * 30:128 * 31], pgb[:, :], AF.Copy).then_inc(c_wfm, 1)
            s.wait_ge(c_gtb, 32)
            s.activation(sb["wfm"][0:8, 128 * 31:128 * 32], pgb[:, :], AF.Copy).then_inc(c_wfm, 1)

    return nc


def kernel(L0_attn_x, L0_attn_w_in, L0_attn_b_in, L0_attn_w_conv, L0_attn_b_conv,
           L0_attn_w_xproj, L0_attn_w_dt, L0_attn_b_dt, L0_attn_w_out,
           L0_ffn_gate_w, L0_ffn_gate_b, L0_ffn_expert_w, L0_ffn_expert_b):
    x = np.ascontiguousarray(np.asarray(L0_attn_x, dtype=np.float32))
    w_in = np.asarray(L0_attn_w_in, dtype=np.float32)
    b_in = np.asarray(L0_attn_b_in, dtype=np.float32)
    w_conv = np.asarray(L0_attn_w_conv, dtype=np.float32)
    b_conv = np.ascontiguousarray(np.asarray(L0_attn_b_conv, dtype=np.float32))
    w_xproj = np.asarray(L0_attn_w_xproj, dtype=np.float32)
    w_dt = np.asarray(L0_attn_w_dt, dtype=np.float32)
    b_dt = np.asarray(L0_attn_b_dt, dtype=np.float32)
    w_out = np.asarray(L0_attn_w_out, dtype=np.float32)
    gate_w = np.ascontiguousarray(np.asarray(L0_ffn_gate_w, dtype=np.float32))
    gate_b = np.asarray(L0_ffn_gate_b, dtype=np.float32)
    expert_w = np.ascontiguousarray(np.asarray(L0_ffn_expert_w, dtype=np.float32))
    expert_b = np.ascontiguousarray(np.asarray(L0_ffn_expert_b, dtype=np.float32))

    if "nc" not in _CACHE:
        _CACHE["nc"] = _build()
    nc = _CACHE["nc"]

    wi = np.ascontiguousarray(w_in[0:2064, :])
    bi = np.zeros(2176, np.float32)
    bi[0:2064] = b_in[0:2064]
    wc = w_conv[:, 0, :]
    diag = np.zeros((16, 4, 128, 128), np.float32)
    ii = np.arange(128)
    for m in range(16):
        for tap in range(4):
            diag[m, tap, ii, ii] = wc[m * 128:(m + 1) * 128, tap]
    wdbl = np.ascontiguousarray(np.concatenate([w_xproj, w_dt], axis=0))
    b48 = np.ascontiguousarray(np.concatenate([np.zeros(32, np.float32), b_dt]).reshape(48, 1))
    wo16 = np.ascontiguousarray(w_out[:, 0:16])
    gb2 = np.ascontiguousarray(gate_b.reshape(8, 1))
    idin = np.eye(128, dtype=np.float32)
    cst = np.tile(np.array([[1.0 / D, EPS, -0.5, -0.125]], np.float32), (128, 1))

    x4 = x.reshape(2, 4, 4096, 1024)
    in_maps = []
    for core in range(8):
        seq, q = core // 4, core % 4
        xs = np.ascontiguousarray(x4[seq, q])
        if q == 0:
            xhs = np.zeros((3, 1024), np.float32)
            hm = np.zeros((128, 1), np.float32)
        else:
            xhs = np.ascontiguousarray(x[seq, q * 4096 - 3:q * 4096, :])
            hm = np.ones((128, 1), np.float32)
        sel = np.zeros((16, 3), np.float32)
        sel[:, 0:q] = 1.0
        selinv = np.ascontiguousarray((1.0 - sel).astype(np.float32))
        in_maps.append(dict(
            x=xs, xh=xhs, hmask=hm, cst=cst, sel=sel, selinv=selinv,
            wi=wi, bi=bi, wcdiag=diag, bc=b_conv, wdbl=wdbl, b48=b48,
            wo16=wo16, gw=gate_w, gb=gb2, ew=expert_w, eb=expert_b, idin=idin,
        ))
    res = run_bass_kernel_spmd(nc, in_maps, core_ids=list(range(8)))
    _CACHE["res"] = res
    out = np.empty((2, 16384, 1024), np.float32)
    for core in range(8):
        seq, q = core // 4, core % 4
        out[seq, q * 4096:(q + 1) * 4096, :] = res.results[core]["out"]
    return out


# version 06
# speedup vs baseline: 1.0309x; 1.0036x over previous
import numpy as np
import concourse.bass as bass
import concourse.mybir as mybir
from concourse.bass_utils import run_bass_kernel_spmd

F32 = mybir.dt.float32
BF16 = mybir.dt.bfloat16
U8 = mybir.dt.uint8
AF = mybir.ActivationFunctionType
ALU = mybir.AluOpType
AX = mybir.AxisListType

TOK = 4096
D = 1024
DI = 2048
DS = 16
NE = 8
T1 = 512
NT1 = TOK // T1
G1 = T1 // 128
NG = 1 + NT1 * G1
T2 = 1024
NT2 = TOK // T2
G2 = T2 // 128
CH = 256
NCH = TOK // CH
EPS = 1e-5
KB = 1024

_CACHE = {}
_DBG = False
_PHASES = "full"
_HALO_MODES = ("H", "Hm", "He", "Heb")
_NT1X = 8


class _DrainInst:

    def __init__(self, eng, ins):
        self._eng = eng
        self._ins = ins

    def then_inc(self, sem_, k=1):
        self._eng.drain().then_inc(sem_, k)
        return self


class _DrainingEngine:
    _RAW = {"wait_ge", "drain", "nop", "dma_start", "then_inc"}

    def __init__(self, eng):
        self._e = eng

    def __getattr__(self, name):
        f = getattr(self._e, name)
        if name in _DrainingEngine._RAW or not callable(f):
            return f

        def wrap(*a, **kw):
            return _DrainInst(self._e, f(*a, **kw))

        return wrap


class QSem:

    def __init__(self, mk, name, nslot):
        self.s = [mk(f"{name}{i}") for i in range(nslot)]
        self.nuse = [0] * nslot
        self.total = 0

    def issue(self, eng, mkdma):
        j = self.total
        sl = j % len(self.s)
        if self.nuse[sl] > 0:
            eng.wait_ge(self.s[sl], 16 * self.nuse[sl])
        mkdma().then_inc(self.s[sl], 16)
        self.nuse[sl] += 1
        self.total += 1
        return j

    def wait_done(self, eng, j):
        assert j >= 0
        eng.wait_ge(self.s[j % len(self.s)], 16 * (j // len(self.s) + 1))

    def wait_all(self, eng, upto):
        for sl in range(len(self.s)):
            cnt = (upto - sl + len(self.s) - 1) // len(self.s)
            if cnt > 0:
                eng.wait_ge(self.s[sl], 16 * cnt)


def _build():
    nc = bass.Bass("TRN2", target_bir_lowering=False, detect_race_conditions=False)

    dr = {}
    dr["x"] = nc.dram_tensor("x", [TOK, D], F32, kind="ExternalInput")
    dr["xh"] = nc.dram_tensor("xh", [3, D], F32, kind="ExternalInput")
    dr["hmask"] = nc.dram_tensor("hmask", [128, 1], F32, kind="ExternalInput")
    dr["cst"] = nc.dram_tensor("cst", [128, 4], F32, kind="ExternalInput")
    dr["sel"] = nc.dram_tensor("sel", [16, 3], F32, kind="ExternalInput")
    dr["selinv"] = nc.dram_tensor("selinv", [16, 3], F32, kind="ExternalInput")
    dr["wi"] = nc.dram_tensor("wi", [2064, D], F32, kind="ExternalInput")
    dr["bi"] = nc.dram_tensor("bi", [2176], F32, kind="ExternalInput")
    dr["wcdiag"] = nc.dram_tensor("wcdiag", [16, 4, 128, 128], F32, kind="ExternalInput")
    dr["bc"] = nc.dram_tensor("bc", [DI], F32, kind="ExternalInput")
    dr["wdbl"] = nc.dram_tensor("wdbl", [48, DI], F32, kind="ExternalInput")
    dr["b48"] = nc.dram_tensor("b48", [48, 1], F32, kind="ExternalInput")
    dr["wo16"] = nc.dram_tensor("wo16", [D, DS], F32, kind="ExternalInput")
    dr["gw"] = nc.dram_tensor("gw", [D, NE], F32, kind="ExternalInput")
    dr["gb"] = nc.dram_tensor("gb", [NE, 1], F32, kind="ExternalInput")
    dr["ew"] = nc.dram_tensor("ew", [NE, D, D], F32, kind="ExternalInput")
    dr["eb"] = nc.dram_tensor("eb", [NE, D], F32, kind="ExternalInput")
    dr["idin"] = nc.dram_tensor("idin", [128, 128], F32, kind="ExternalInput")
    dr["out"] = nc.dram_tensor("out", [TOK, D], F32, kind="ExternalOutput")
    stg_kind = "ExternalOutput" if _DBG else "Internal"
    dr["xn_dram"] = nc.dram_tensor("xn_dram", [8, 128, TOK], BF16, kind=stg_kind)
    dr["dbl_dram"] = nc.dram_tensor("dbl_dram", [48, TOK], F32, kind=stg_kind)
    dr["z16_dram"] = nc.dram_tensor("z16_dram", [16, TOK], F32, kind=stg_kind)
    dr["x16_dram"] = nc.dram_tensor("x16_dram", [16, TOK], F32, kind=stg_kind)
    dr["gate_dram"] = nc.dram_tensor("gate_dram", [8, TOK], F32, kind=stg_kind)
    dr["ag_in"] = nc.dram_tensor("ag_in", [16, 2], F32)
    dr["ag_out"] = nc.dram_tensor("ag_out", [4, 16, 2], F32)
    if _DBG:
        dr["d_wtm"] = nc.dram_tensor("d_wtm", [128, 256], F32, kind="ExternalOutput")
        dr["d_wfm"] = nc.dram_tensor("d_wfm", [8, TOK], BF16, kind="ExternalOutput")
        dr["d_hz"] = nc.dram_tensor("d_hz", [16, TOK], BF16, kind="ExternalOutput")
        dr["d_hloc"] = nc.dram_tensor("d_hloc", [16, TOK], F32, kind="ExternalOutput")
        dr["d_cf"] = nc.dram_tensor("d_cf", [16, 8], F32, kind="ExternalOutput")
        dr["d_gath"] = nc.dram_tensor("d_gath", [16, 8], F32, kind="ExternalOutput")
        dr["d_lnb"] = nc.dram_tensor("d_lnb", [128, 16], F32, kind="ExternalOutput")

    pend = []

    class Region:
        def __init__(self, base):
            self.off = base

        def take(self, name, shape, dtype):
            per = int(np.prod(shape[1:])) * mybir.dt.size(dtype)
            per = (per + 31) & ~31
            pend.append((name, shape, dtype, self.off))
            self.off += per

    P = Region(0)
    for nm, sh, dt in [
        ("idbf", [128, 128], BF16), ("idf32", [128, 128], F32),
        ("bi_sb", [128, 17], F32), ("bc_sb", [128, 16], F32),
        ("b48_sb", [48, 1], F32), ("gb_sb", [8, 1], F32),
        ("hm_sb", [128, 1], F32), ("cst_sb", [128, 4], F32),
        ("sel_sb", [16, 3], F32),
        ("seli_sb", [16, 3], F32), ("eb_sb", [8, D], BF16),
        ("gw_sb", [128, 8, 8], BF16), ("wo16m", [16, D], BF16),
        ("wfm", [8, TOK], BF16), ("wtm", [128, 256], F32),
        ("hz_bf", [16, TOK], BF16), ("haloxz", [128, 48], BF16),
        ("xnh_fm", [128, 8, 3], BF16), ("tmp48", [128, 48], F32),
    ]:
        P.take(nm, sh, dt)
    pbase = P.off

    A = Region(pbase)
    for nm, sh, dt in [
        ("wi_lhsT", [128, 8, 2064], BF16), ("diag_sb", [128, 16, 4, 128], BF16),
        ("wdbl_lhsT", [128, 16, 48], BF16),
        ("wstage0", [128, D], BF16), ("wstage1", [128, D], BF16),
        ("wdblst", [48, DI], BF16), ("wo16st", [128, 128], BF16),
        ("xbuf0", [128, G1 * D], F32), ("xbuf1", [128, G1 * D], F32),
        ("lnbuf", [128, 16], F32), ("sqdump", [128, D], F32),
        ("xn_tm0", [128, D], BF16), ("xn_tm1", [128, D], BF16),
        ("xnf0", [128, 8, T1], BF16), ("xnf1", [128, 8, T1], BF16),
        ("xn1b0", [128, 8, T1], BF16), ("xn1b1", [128, 8, T1], BF16),
        ("cbuf0", [128, 515], BF16), ("cbuf1", [128, 515], BF16),
        ("xcb0", [128, T1], BF16), ("xcb1", [128, T1], BF16),
        ("dstg0", [48, T1], F32), ("dstg1", [48, T1], F32),
        ("gstg0", [8, T1], F32), ("gstg1", [8, T1], F32),
        ("zstg0", [16, T1], F32), ("zstg1", [16, T1], F32),
        ("xstg0", [16, T1], F32), ("xstg1", [16, T1], F32),
    ]:
        A.take(nm, sh, dt)

    B = Region(pbase)
    blist = [
        ("xn2b0", [128, 8, T2], BF16),
        ("ebuf0", [128, 8, D], BF16), ("ebuf1", [128, 8, D], BF16),
        ("x2r0", [128, D], F32), ("x2r1", [128, D], F32), ("x2r2", [128, D], F32),
        ("acc", [128, 8, D], F32),
        ("zeros16", [16, CH], F32),
        ("h_loc", [16, TOK], F32), ("cumP", [16, TOK], F32),
        ("gfull", [8, TOK], F32),
        ("glog", [128, 8], F32), ("esb", [128, 8], F32), ("psb", [128, 8], F32),
        ("msk", [128, 8], F32), ("sft", [128, 4], F32), ("sftn", [128, 1], F32),
        ("gmx", [128, 8], F32),
        ("smry", [16, 2], F32), ("gath", [16, 8], F32), ("cfold", [16, 8], F32),
    ]
    for nm in ["inA", "inB", "inD", "inX", "tS1", "tS2", "tD", "tV1", "Abf", "Bbf", "zb"]:
        for p in range(2):
            blist.append((f"{nm}{p}", [16, CH], F32))
    for nm, sh, dt in blist:
        B.take(nm, sh, dt)

    total = max(A.off, B.off)
    assert total <= 196 * KB, f"SBUF budget blown: {total / KB:.1f} KB"

    sb = {}
    with nc.sbuf_tensor("arena", [128, total], U8) as arena:
        base_addr = nc.lookup_mloc(arena).addr
        for name, shape, dtype, off in pend:
            sb[name] = nc.alloc_sbuf_tensor_at(name, shape, dtype, offset=base_addr + off)
        _prog(nc, sb, dr)
    return nc


def _prog(nc, sb, dr):
    PH = _PHASES
    from contextlib import ExitStack
    with ExitStack() as st:
        P01 = st.enter_context(nc.psum_tensor("P01", [128, 1024], F32))
        P23 = st.enter_context(nc.psum_tensor("P23", [128, 1024], F32))
        P4 = st.enter_context(nc.psum_tensor("P4", [128, 512], F32))
        P5 = st.enter_context(nc.psum_tensor("P5", [128, 512], F32))
        P6 = st.enter_context(nc.psum_tensor("P6", [128, 512], F32))
        P7 = st.enter_context(nc.psum_tensor("P7", [128, 512], F32))

        def sem(name):
            return st.enter_context(nc.semaphore(name))

        QX = QSem(sem, "qx", 2)
        QWLD = QSem(sem, "qw", 3)
        QXND = QSem(sem, "qn", 2)
        QX1 = QSem(sem, "q1", 2)
        QST = QSem(sem, "qs", 2)
        QSS = QSem(sem, "qp", 3)
        QEB = QSem(sem, "qe", 2)
        QX2 = QSem(sem, "q2", 3)
        QXN2 = QSem(sem, "qq", 2)
        QOUT = QSem(sem, "qo", 2)

        c_lns, c_tr, c_evV, c_evS = sem("c_lns"), sem("c_tr"), sem("c_evV"), sem("c_evS")
        c_mmxz, c_evxz, c_cv, c_sil = sem("c_mmxz"), sem("c_evxz"), sem("c_cv"), sem("c_sil")
        c_dblm, c_evdbl, c_gate, c_evgate = sem("c_dblm"), sem("c_evdbl"), sem("c_gate"), sem("c_evgate")
        c_hin, c_sp, c_pp, c_scan = sem("c_hin"), sem("c_sp"), sem("c_pp"), sem("c_scan")
        c_smy, c_cc, c_gvv, c_gtf = sem("c_smy"), sem("c_cc"), sem("c_gvv"), sem("c_gtf")
        c_gse, c_gex = sem("c_gse"), sem("c_gex")
        c_gtb, c_wfm, c_hzc, c_mmG, c_mmE = sem("c_gtb"), sem("c_wfm"), sem("c_hzc"), sem("c_mmG"), sem("c_mmE")
        c_stt, c_stt0, c_mmM, c_madd = sem("c_stt"), sem("c_stt0"), sem("c_mmM"), sem("c_madd")
        block = st.enter_context(nc.Block())
        pxz = [P4, P5]
        pcv = [P6, P7]
        pdbl = P01[0:48, 0:512]
        pgate = P01[0:8, 512:1024]
        p23bf = P23.bitcast(BF16)
        ptr = [p23bf[:, 0:128], p23bf[:, 1024:1152]]
        pe = [P01, P23]
        pm = [P4, P5]
        pgf = P6[:, 0:8]
        pgb = P7[0:8, 0:128]

        TR = [("xn", 0, k) for k in range(8)]
        for i in range(1, NG):
            TR += [("xn", i, k) for k in range(8)]
            if i == 2:
                TR += [("wo16", 0, k) for k in range(8)]
            elif i == 3:
                TR += [("wdbl", 0, k) for k in range(8)]
            elif i == 4:
                TR += [("wdbl", 0, k) for k in range(8, 16)]
            elif 5 <= i <= 21:
                TR += [("wi", i - 5, k) for k in range(8)]
        nV = nS = 0
        plan = []
        last_tr_of_group = {}
        last_tr_of_wichunk = {}
        last_evV_of_group = {}
        for n, (kind, idx, k) in enumerate(TR):
            if kind == "xn":
                nV += 1
                ev = ("V", nV)
                last_tr_of_group[idx] = n + 1
                last_evV_of_group[idx] = nV
            else:
                nS += 1
                ev = ("S", nS)
                if kind == "wi":
                    last_tr_of_wichunk[idx] = n + 1
            plan.append((kind, idx, k, ev))
        N_EVS = nS

        def gtile(i):
            return (i - 1) // G1

        (W_IDBF, W_IDF, W_BI, W_BC, W_B48, W_GB, W_HM, W_CST, W_SEL,
         W_SELI, W_EB, W_GW, W_WO, W_WDBL, W_DGA, W_DGB) = range(16)

        def w_wi(r):
            return 16 + r

        W_TOT = w_wi(16) + 1

        def evxz_cnt(t, m):
            return (m + 1) if t < 0 else (16 + 17 * t + m + 1)

        def st_idx(t, j):
            return 4 * t + j

        SS_GF = 0

        def ss_chunk(c, j):
            return 1 + 4 * c + j

        SS_AGIN = 1 + 4 * NCH
        SS_GATH = 2 + 4 * NCH

        def ss_z(c):
            return 3 + 4 * NCH + c

        @block.sync
        def _(sy):
            def ldx(t):
                if t < 0:
                    QX.issue(sy, lambda: sy.dma_start(sb["xbuf0"][0:3, 0:D], dr["xh"][:, :]))
                    return
                slot = sb["xbuf1"] if t % 2 == 0 else sb["xbuf0"]
                if t == 1:
                    sy.wait_ge(c_lns, 1)
                elif t >= 2:
                    sy.wait_ge(c_lns, 4 * (t - 2) + 5)
                src = dr["x"][t * T1:(t + 1) * T1, :].rearrange("(g p) f -> p g f", p=128)
                QX.issue(sy, lambda: sy.dma_start(
                    slot[:, :].rearrange("p (g f) -> p g f", g=G1), src))

            ldx(-1)
            ldx(0)
            ldx(1)
            for t in range(NT1):
                sy.wait_ge(c_evV, last_evV_of_group[1 + 4 * t + 3])
                xnf = sb["xnf0"] if t % 2 == 0 else sb["xnf1"]
                dst = dr["xn_dram"][:, :, t * T1:(t + 1) * T1].rearrange("k p c -> p k c")
                QXND.issue(sy, lambda: sy.dma_start(dst, xnf[:, :, :]))
                if t + 2 < NT1:
                    ldx(t + 2)

            if PH == "0":
                QXND.wait_all(sy, NT1)
                if _DBG:
                    QOUT.issue(sy, lambda: sy.dma_start(dr["d_lnb"][:, :], sb["lnbuf"][:, :]))
                    QOUT.wait_all(sy, 1)
                return

            def stores(t):
                par = t % 2
                sy.wait_ge(c_sil, 16 * t + 1)
                QST.issue(sy, lambda: sy.dma_start(
                    dr["x16_dram"][:, t * T1:(t + 1) * T1], sb[f"xstg{par}"][:, :]))
                sy.wait_ge(c_evxz, evxz_cnt(t, 16))
                QST.issue(sy, lambda: sy.dma_start(
                    dr["z16_dram"][:, t * T1:(t + 1) * T1], sb[f"zstg{par}"][:, :]))
                sy.wait_ge(c_evdbl, t + 1)
                QST.issue(sy, lambda: sy.dma_start(
                    dr["dbl_dram"][:, t * T1:(t + 1) * T1], sb[f"dstg{par}"][:, :]))
                sy.wait_ge(c_evgate, t + 1)
                QST.issue(sy, lambda: sy.dma_start(
                    dr["gate_dram"][:, t * T1:(t + 1) * T1], sb[f"gstg{par}"][:, :]))

            if PH in _HALO_MODES:
                QXND.wait_all(sy, NT1)
                return
            NTX = _NT1X
            for t in range(NTX):
                if t >= 2:
                    stores(t - 2)
                QXND.wait_done(sy, t)
                if t >= 2:
                    sy.wait_ge(c_gate, t - 1)
                xn1 = sb["xn1b0"] if t % 2 == 0 else sb["xn1b1"]
                src = dr["xn_dram"][:, :, t * T1:(t + 1) * T1].rearrange("k p c -> p k c")
                QX1.issue(sy, lambda: sy.dma_start(xn1[:, :, :], src))
            if NTX >= 2:
                stores(NTX - 2)
            if NTX >= 1:
                stores(NTX - 1)
            if NTX < NT1:
                QST.wait_all(sy, 4 * NTX)
                return
            if PH == "01":
                QST.wait_all(sy, 4 * NT1)
                return
            sy.wait_ge(c_evgate, NT1)
            sy.wait_ge(c_gate, NT1)
            QST.wait_all(sy, 4 * NT1)
            QSS.issue(sy, lambda: sy.dma_start(sb["gfull"][:, :], dr["gate_dram"][:, :]))
            if PH == "full":
                src0 = dr["xn_dram"][:, :, 0:T2].rearrange("k p c -> p k c")
                QXN2.issue(sy, lambda: sy.dma_start(sb["xn2b0"][:, :, :], src0))
                for g in range(3):
                    QX2.issue(sy, lambda: sy.dma_start(
                        sb[f"x2r{g % 3}"][:, :], dr["x"][g * 128:(g + 1) * 128, :]))
            for c in range(NCH):
                p = c % 2
                if c >= 2:
                    sy.wait_ge(c_scan, 2 * (c - 2) + 2)
                QSS.issue(sy, lambda: sy.dma_start(
                    sb[f"inA{p}"][:, :], dr["dbl_dram"][0:16, c * CH:(c + 1) * CH]))
                QSS.issue(sy, lambda: sy.dma_start(
                    sb[f"inB{p}"][:, :], dr["dbl_dram"][16:32, c * CH:(c + 1) * CH]))
                QSS.issue(sy, lambda: sy.dma_start(
                    sb[f"inD{p}"][:, :], dr["dbl_dram"][32:48, c * CH:(c + 1) * CH]))
                QSS.issue(sy, lambda: sy.dma_start(
                    sb[f"inX{p}"][:, :], dr["x16_dram"][:, c * CH:(c + 1) * CH]))
            sy.wait_ge(c_smy, 2)
            QSS.issue(sy, lambda: sy.dma_start(dr["ag_in"][:, :], sb["smry"][:, :]))
            if PH == "full":
                for g in range(3, G2):
                    sy.wait_ge(c_stt0, g - 2)
                    QX2.issue(sy, lambda: sy.dma_start(
                        sb[f"x2r{g % 3}"][:, :], dr["x"][g * 128:(g + 1) * 128, :]))
            sy.wait_ge(c_smy, 3)
            QSS.issue(sy, lambda: sy.dma_start(
                sb["gath"][:, :].rearrange("p (r c) -> p r c", r=4),
                dr["ag_out"][:, :, :].rearrange("r p c -> p r c")))
            for c in range(NCH):
                if c >= 2:
                    sy.wait_ge(c_hzc, 2 * (c - 2) + 2)
                QSS.issue(sy, lambda: sy.dma_start(
                    sb[f"zb{c % 2}"][:, :], dr["z16_dram"][:, c * CH:(c + 1) * CH]))
            if PH == "01S":
                QSS.wait_all(sy, 3 + 5 * NCH)
                sy.wait_ge(c_hzc, 2 * NCH)
                return
            for t in range(1, NT2):
                sy.wait_ge(c_mmE, 8 * t)
                src2 = dr["xn_dram"][:, :, t * T2:(t + 1) * T2].rearrange("k p c -> p k c")
                QXN2.issue(sy, lambda: sy.dma_start(sb["xn2b0"][:, :, :], src2))
                sy.wait_ge(c_madd, 16 * (t - 1) + 16)
                dst1 = dr["out"][(t - 1) * T2:t * T2, :].rearrange("(g p) f -> p g f", p=128)
                QOUT.issue(sy, lambda: sy.dma_start(dst1, sb["acc"][:, :, :]))
                for g in range(G2):
                    gx = 8 * t + g
                    sy.wait_ge(c_stt0, max(0, gx - 2))
                    QX2.issue(sy, lambda: sy.dma_start(
                        sb[f"x2r{gx % 3}"][:, :],
                        dr["x"][t * T2 + g * 128:t * T2 + (g + 1) * 128, :]))
            sy.wait_ge(c_madd, 16 * (NT2 - 1) + 16)
            dst2 = dr["out"][(NT2 - 1) * T2:NT2 * T2, :].rearrange("(g p) f -> p g f", p=128)
            QOUT.issue(sy, lambda: sy.dma_start(dst2, sb["acc"][:, :, :]))
            QOUT.wait_all(sy, NT2)
            if _DBG:
                for nmA, nmB in [("d_wtm", "wtm"), ("d_wfm", "wfm"), ("d_hz", "hz_bf"),
                                 ("d_hloc", "h_loc"), ("d_cf", "cfold"), ("d_gath", "gath")]:
                    QOUT.issue(sy, lambda nmA=nmA, nmB=nmB: sy.dma_start(
                        dr[nmA][:, :], sb[nmB][:, :]))
                QOUT.wait_all(sy, NT2 + 6)

        @block.gpsimd
        def _(g):
            def ld(mk):
                QWLD.issue(g, mk)

            ld(lambda: g.dma_start(sb["idbf"][:, :], dr["idin"][:, :]))
            ld(lambda: g.dma_start(sb["idf32"][:, :], dr["idin"][:, :]))
            ld(lambda: g.dma_start(sb["bi_sb"][:, :], dr["bi"][:].rearrange("(m p) -> p m", p=128),
                                   allow_slow_non_contiguous=True))
            ld(lambda: g.dma_start(sb["bc_sb"][:, :], dr["bc"][:].rearrange("(m p) -> p m", p=128),
                                   allow_slow_non_contiguous=True))
            ld(lambda: g.dma_start(sb["b48_sb"][:, :], dr["b48"][:, :]))
            ld(lambda: g.dma_start(sb["gb_sb"][:, :], dr["gb"][:, :]))
            ld(lambda: g.dma_start(sb["hm_sb"][:, :], dr["hmask"][:, :]))
            ld(lambda: g.dma_start(sb["cst_sb"][:, :], dr["cst"][:, :]))
            ld(lambda: g.dma_start(sb["sel_sb"][:, :], dr["sel"][:, :]))
            ld(lambda: g.dma_start(sb["seli_sb"][:, :], dr["selinv"][:, :]))
            ld(lambda: g.dma_start(sb["eb_sb"][:, :], dr["eb"][:, :]))
            ld(lambda: g.dma_start(sb["gw_sb"][:, :, :],
                                   dr["gw"][:, :].rearrange("(k p) e -> p k e", p=128)))
            ld(lambda: g.dma_start(sb["wo16st"][:, :].rearrange("p (k s) -> p k s", k=8),
                                   dr["wo16"][:, :].rearrange("(k p) s -> p k s", p=128)))
            ld(lambda: g.dma_start(sb["wdblst"][:, :], dr["wdbl"][:, :]))
            ld(lambda: g.dma_start(sb["diag_sb"][:, 0:8, :, :],
                                   dr["wcdiag"][0:8, :, :, :].rearrange("m t i j -> i m t j")))
            ld(lambda: g.dma_start(sb["diag_sb"][:, 8:16, :, :],
                                   dr["wcdiag"][8:16, :, :, :].rearrange("m t i j -> i m t j")))
            for r in range(17):
                if r >= 2:
                    g.wait_ge(c_tr, last_tr_of_wichunk[r - 2])
                stg = sb["wstage0"] if r % 2 == 0 else sb["wstage1"]
                if r < 16:
                    ld(lambda: g.dma_start(stg[:, :], dr["wi"][r * 128:(r + 1) * 128, :]))
                else:
                    ld(lambda: g.dma_start(stg[0:16, :], dr["wi"][2048:2064, :]))
            if PH in ("0", "01") or PH in _HALO_MODES or _NT1X < NT1:
                return
            g.wait_ge(c_gate, NT1)
            g.wait_ge(c_evgate, NT1)
            QXND.wait_all(g, NT1)
            QST.wait_all(g, 4 * NT1)
            g.wait_ge(c_evxz, evxz_cnt(NT1 - 1, 16))
            g.wait_ge(c_sil, 16 * NT1)

            def eb_load(idx):
                if idx >= 2:
                    g.wait_ge(c_mmE, idx - 1)
                eb = sb["ebuf0"] if idx % 2 == 0 else sb["ebuf1"]
                e = idx % NE
                src = dr["ew"][e, :, :].rearrange("(k p) n -> p k n", p=128)
                QEB.issue(g, lambda: g.dma_start(eb[:, :, :], src))

            def chunk_pool(c):
                p = c % 2
                if c >= 2:
                    g.wait_ge(c_sp, 5 * (c - 2) + 5)
                QSS.wait_done(g, ss_chunk(c, 2))
                g.tensor_single_scalar(sb[f"tV1{p}"][:, :], sb[f"inD{p}"][:, :], 0.0, ALU.max)
                g.drain().then_inc(c_pp, 1)
                g.wait_ge(c_sp, 5 * c + 3)
                g.tensor_add(sb[f"tD{p}"][:, :], sb[f"tS2{p}"][:, :], sb[f"tV1{p}"][:, :])
                g.drain().then_inc(c_pp, 1)
                g.wait_ge(c_sp, 5 * c + 4)
                g.tensor_mul(sb[f"tV1{p}"][:, :], sb[f"tS1{p}"][:, :], sb[f"tD{p}"][:, :])
                g.drain().then_inc(c_pp, 1)
                QSS.wait_done(g, ss_chunk(c, 1))
                g.tensor_mul(sb[f"tS2{p}"][:, :], sb[f"inB{p}"][:, :], sb[f"tD{p}"][:, :])
                g.drain().then_inc(c_pp, 1)
                QSS.wait_done(g, ss_chunk(c, 3))
                g.tensor_mul(sb[f"Bbf{p}"][:, :], sb[f"tS2{p}"][:, :], sb[f"inX{p}"][:, :])
                g.drain().then_inc(c_pp, 1)

            if PH == "full":
                eb_load(0)
                eb_load(1)
            for c in range(NCH):
                chunk_pool(c)
            if PH == "01S":
                QSS.wait_done(g, SS_AGIN)
                g.collective_compute(
                    "AllGather", ALU.bypass,
                    replica_groups=[[0, 1, 2, 3], [4, 5, 6, 7]],
                    ins=[dr["ag_in"][:, :].opt()],
                    outs=[dr["ag_out"][:, :, :].opt()],
                ).then_inc(c_cc, 1)
                return
            for idx in range(2, NT2 * NE):
                eb_load(idx)
                if idx == 4:
                    QSS.wait_done(g, SS_AGIN)
                    g.collective_compute(
                        "AllGather", ALU.bypass,
                        replica_groups=[[0, 1, 2, 3], [4, 5, 6, 7]],
                        ins=[dr["ag_in"][:, :].opt()],
                        outs=[dr["ag_out"][:, :, :].opt()],
                    ).then_inc(c_cc, 1)

        def _dblmm(t_, t, m):
            t_.wait_ge(c_sil, 16 * t + m + 1)
            if m == 0:
                t_.wait_ge(c_evdbl, t)
            xc = sb["xcb0"] if m % 2 == 0 else sb["xcb1"]
            t_.matmul(pdbl[:, :], sb["wdbl_lhsT"][:, m, :], xc[:, :],
                      start=(m == 0), stop=(m == 15))
            t_.drain().then_inc(c_dblm, 1)

        @block.tensor
        def _(t_):
            for n, (kind, idx, k, ev) in enumerate(plan):
                if n >= 2:
                    pev = plan[n - 2][3]
                    t_.wait_ge(c_evV if pev[0] == "V" else c_evS, pev[1])
                dst = ptr[n % 2]
                if kind == "xn":
                    if k == 0:
                        t_.wait_ge(c_lns, idx + 1)
                    src_t = sb["xn_tm0"] if idx % 2 == 0 else sb["xn_tm1"]
                    rows = 3 if idx == 0 else 128
                    t_.transpose(dst[0:128, 0:rows], src_t[0:rows, k * 128:(k + 1) * 128],
                                 sb["idbf"][0:rows, 0:rows]).then_inc(c_tr, 1)
                elif kind == "wi":
                    if k == 0:
                        QWLD.wait_done(t_, w_wi(idx))
                    stg = sb["wstage0"] if idx % 2 == 0 else sb["wstage1"]
                    rows = 128 if idx < 16 else 16
                    t_.transpose(dst[0:128, 0:rows], stg[0:rows, k * 128:(k + 1) * 128],
                                 sb["idbf"][0:rows, 0:rows]).then_inc(c_tr, 1)
                elif kind == "wdbl":
                    if k in (0, 8):
                        QWLD.wait_done(t_, W_WDBL)
                    t_.transpose(dst[0:128, 0:48], sb["wdblst"][0:48, k * 128:(k + 1) * 128],
                                 sb["idbf"][0:48, 0:48]).then_inc(c_tr, 1)
                else:
                    if k == 0:
                        QWLD.wait_done(t_, W_WO)
                    t_.transpose(dst[0:16, 0:128], sb["wo16st"][:, k * 16:(k + 1) * 16],
                                 sb["idbf"][:, :]).then_inc(c_tr, 1)

            if PH == "0":
                return
            QWLD.wait_all(t_, W_TOT)
            t_.wait_ge(c_evS, N_EVS)
            t_.wait_ge(c_evV, 8 * NG)
            for m in range(16):
                if m >= 2 and PH != "Hm":
                    t_.wait_ge(c_evxz, m - 1)
                ps = pxz[m % 2]
                for k in range(8):
                    t_.matmul(ps[:, 0:3], sb["wi_lhsT"][:, k, m * 128:(m + 1) * 128],
                              sb["xnh_fm"][:, k, :], start=(k == 0), stop=(k == 7))
                t_.drain().then_inc(c_mmxz, 1)

            if PH in _HALO_MODES:
                return
            for t in range(_NT1X):
                QX1.wait_done(t_, t)
                xn1 = sb["xn1b0"] if t % 2 == 0 else sb["xn1b1"]
                for m in range(17):
                    u = 17 * t + m
                    t_.wait_ge(c_evxz, evxz_cnt(t, m) - 2)
                    ps = pxz[(16 + u) % 2]
                    rows = 128 if m < 16 else 16
                    for k in range(8):
                        t_.matmul(ps[0:rows, :], sb["wi_lhsT"][:, k, m * 128:m * 128 + rows],
                                  xn1[:, k, :], start=(k == 0), stop=(k == 7))
                    t_.drain().then_inc(c_mmxz, 1)
                    if m >= 1:
                        mm = m - 1
                        uc = 16 * t + mm
                        if t == 0 and mm < 2:
                            QWLD.wait_done(t_, W_DGA)
                            QWLD.wait_done(t_, W_DGB)
                        t_.wait_ge(c_sil, max(0, uc - 1))
                        t_.wait_ge(c_evxz, evxz_cnt(t, mm))
                        t_.wait_ge(c_hin, uc + 1)
                        pc = pcv[uc % 2]
                        cb = sb["cbuf0"] if mm % 2 == 0 else sb["cbuf1"]
                        for tap in range(4):
                            t_.matmul(pc[:, :], sb["diag_sb"][:, mm, tap, :],
                                      cb[:, tap:tap + 512], start=(tap == 0), stop=(tap == 3))
                        t_.drain().then_inc(c_cv, 1)
                        if mm >= 1:
                            _dblmm(t_, t, mm - 1)
                _dblmm(t_, t, 15)
                t_.wait_ge(c_evgate, t)
                for k in range(8):
                    t_.matmul(pgate[:, :], sb["gw_sb"][:, k, :], xn1[:, k, :],
                              start=(k == 0), stop=(k == 7))
                t_.drain().then_inc(c_gate, 1)

            if PH in ("01",) or _NT1X < NT1:
                return
            t_.wait_ge(c_evdbl, NT1)
            t_.wait_ge(c_evgate, NT1)
            t_.wait_ge(c_evxz, evxz_cnt(NT1 - 1, 16))
            t_.wait_ge(c_sil, 16 * NT1)
            QSS.wait_done(t_, SS_GF)

            for gg in range(32):
                if gg >= 1:
                    t_.wait_ge(c_gse, gg)
                t_.transpose(pgf[:, :], sb["gfull"][:, gg * 128:(gg + 1) * 128],
                             sb["idf32"][0:8, 0:8])
                t_.drain().then_inc(c_gtf, 1)
                t_.wait_ge(c_gvv, 8 * gg + 8)
                if gg >= 1:
                    t_.wait_ge(c_wfm, gg)
                t_.transpose(pgb[:, :], sb["wtm"][:, 8 * gg:8 * (gg + 1)],
                             sb["idf32"][:, :])
                t_.drain().then_inc(c_gtb, 1)
            if PH == "01S":
                return
            for t in range(NT2):
                QXN2.wait_done(t_, t)
                xn2 = sb["xn2b0"]
                for e in range(NE):
                    QEB.wait_done(t_, 8 * t + e)
                    eb = sb["ebuf0"] if (8 * t + e) % 2 == 0 else sb["ebuf1"]
                    for g_ in range(G2):
                        geidx = t * 64 + e * 8 + g_
                        pp = pe[geidx % 2]
                        for nh in range(2):
                            if geidx >= 2:
                                t_.wait_ge(c_stt, 2 * (geidx - 2) + nh + 1)
                            for k in range(8):
                                t_.matmul(pp[:, nh * 512:(nh + 1) * 512],
                                          xn2[:, k, g_ * 128:(g_ + 1) * 128],
                                          eb[:, k, nh * 512:(nh + 1) * 512],
                                          start=(k == 0), stop=(k == 7))
                            t_.drain().then_inc(c_mmG, 1)
                    t_.drain().then_inc(c_mmE, 1)
                if t == 0:
                    t_.wait_ge(c_hzc, 2 * NCH)
                    t_.wait_ge(c_wfm, 32)
                for g_ in range(G2):
                    for nh in range(2):
                        mcnt = 16 * t + 2 * g_ + nh + 1
                        t_.wait_ge(c_madd, max(0, mcnt - 2))
                        pmx = pm[(2 * g_ + nh) % 2]
                        c0 = t * T2 + g_ * 128
                        t_.matmul(pmx[:, :], sb["hz_bf"][:, c0:c0 + 128],
                                  sb["wo16m"][:, nh * 512:(nh + 1) * 512],
                                  start=True, stop=False)
                        t_.matmul(pmx[:, :], sb["wfm"][:, c0:c0 + 128],
                                  sb["eb_sb"][:, nh * 512:(nh + 1) * 512],
                                  start=False, stop=True)
                        t_.drain().then_inc(c_mmM, 1)

        @block.vector
        def _(v):
            v = _DrainingEngine(v)

            def vspace(n=1):
                for _i in range(n):
                    v.tensor_copy(sb["sft"][0:16, 3:4], sb["zeros16"][:, 0:1])

            for n, (kind, idx, k, ev) in enumerate(plan):
                if ev[0] != "V":
                    continue
                i = idx
                rows = 3 if i == 0 else 128
                if i >= 1:
                    t = gtile(i)
                    g = (i - 1) % G1
                    if g == 0 and k == 0 and t >= 2:
                        QXND.wait_done(v, t - 2)
                    dstb = sb["xnf0"] if t % 2 == 0 else sb["xnf1"]
                    dstp = dstb[:, k, g * 128:g * 128 + rows]
                else:
                    dstp = sb["xnh_fm"][:, k, :]
                v.wait_ge(c_tr, n + 1)
                v.tensor_copy(dstp, ptr[n % 2][0:128, 0:rows]).then_inc(c_evV, 1)

            if PH in ("0", "01") or PH in _HALO_MODES or _NT1X < NT1:
                return
            v.wait_ge(c_evgate, NT1)
            v.wait_ge(c_gate, NT1)
            QST.wait_all(v, 4 * NT1)
            v.memset(sb["zeros16"][:, :], 0.0)

            def scan_chunk(c):
                p = c % 2
                v.wait_ge(c_sp, 5 * c + 5)
                v.wait_ge(c_pp, 5 * c + 5)
                vspace()
                ini_h = 0.0 if c == 0 else sb["h_loc"][:, c * CH - 1:c * CH]
                ini_p = 1.0 if c == 0 else sb["cumP"][:, c * CH - 1:c * CH]
                v.tensor_tensor_scan(sb["h_loc"][:, c * CH:(c + 1) * CH], sb[f"Abf{p}"][:, :],
                                     sb[f"Bbf{p}"][:, :], ini_h, ALU.mult, ALU.add).then_inc(c_scan, 1)
                v.tensor_tensor_scan(sb["cumP"][:, c * CH:(c + 1) * CH], sb[f"Abf{p}"][:, :],
                                     sb["zeros16"][:, :], ini_p, ALU.mult, ALU.add).then_inc(c_scan, 1)

            for gg in range(32):
                v.wait_ge(c_gex, gg + 1)
                v.reduce_sum(sb["sft"][:, 1:2], sb["esb"][:, :], axis=AX.X).then_inc(c_gvv, 1)
                v.reciprocal(sb["sft"][:, 2:3], sb["sft"][:, 1:2]).then_inc(c_gvv, 1)
                vspace()
                v.tensor_single_scalar(sb["psb"][:, :], sb["esb"][:, :], sb["sft"][:, 2:3], ALU.mult).then_inc(c_gvv, 1)
                v.tensor_single_scalar(sb["msk"][:, :], sb["psb"][:, :], 0.1, ALU.is_ge).then_inc(c_gvv, 1)
                v.tensor_mul(sb["psb"][:, :], sb["psb"][:, :], sb["msk"][:, :]).then_inc(c_gvv, 1)
                v.reduce_sum(sb["sft"][:, 1:2], sb["psb"][:, :], axis=AX.X).then_inc(c_gvv, 1)
                v.reciprocal(sb["sft"][:, 2:3], sb["sft"][:, 1:2]).then_inc(c_gvv, 1)
                vspace()
                v.tensor_single_scalar(sb["wtm"][:, 8 * gg:8 * (gg + 1)], sb["psb"][:, :],
                                       sb["sft"][:, 2:3], ALU.mult).then_inc(c_gvv, 1)
                if gg % 2 == 1:
                    scan_chunk(gg // 2)

            def stts(t, e):
                for g_ in range(G2):
                    geidx = t * 64 + e * 8 + g_
                    gx = 8 * t + g_
                    pp = pe[geidx % 2]
                    for nh in range(2):
                        v.wait_ge(c_mmG, 2 * geidx + nh + 1)
                        accsl = sb["acc"][:, g_, nh * 512:(nh + 1) * 512]
                        if e == 0:
                            QX2.wait_done(v, gx)
                            in1 = sb[f"x2r{gx % 3}"][:, nh * 512:(nh + 1) * 512]
                        else:
                            in1 = accsl
                        wcol = sb["wtm"][:, gx * 8 + e:gx * 8 + e + 1]
                        v.scalar_tensor_tensor(accsl, pp[:, nh * 512:(nh + 1) * 512],
                                               wcol, in1, ALU.mult, ALU.add).then_inc(c_stt, 1)
                        if e == 0 and nh == 1:
                            v.nop().then_inc(c_stt0, 1)

            def madds(t):
                for g_ in range(G2):
                    for nh in range(2):
                        v.wait_ge(c_mmM, 16 * t + 2 * g_ + nh + 1)
                        accsl = sb["acc"][:, g_, nh * 512:(nh + 1) * 512]
                        v.tensor_add(accsl, accsl, pm[(2 * g_ + nh) % 2][:, :]).then_inc(c_madd, 1)

            v.tensor_copy(sb["smry"][:, 0:1], sb["cumP"][:, TOK - 1:TOK]).then_inc(c_smy, 1)
            v.tensor_copy(sb["smry"][:, 1:2], sb["h_loc"][:, TOK - 1:TOK]).then_inc(c_smy, 1)
            if PH == "full":
                stts(0, 0)
                stts(0, 1)
                stts(0, 2)
                stts(0, 3)
            v.wait_ge(c_cc, 1)
            v.nop().then_inc(c_smy, 1)
            QSS.wait_done(v, SS_GATH)
            QWLD.wait_done(v, W_SEL)
            QWLD.wait_done(v, W_SELI)
            C = sb["cfold"]
            v.memset(C[:, 0:1], 0.0)
            vspace(2)
            for j in range(3):
                v.tensor_single_scalar(C[:, 2:3], sb["gath"][:, 2 * j:2 * j + 1], sb["sel_sb"][:, j:j + 1], ALU.mult)
                vspace(2)
                v.tensor_add(C[:, 2:3], C[:, 2:3], sb["seli_sb"][:, j:j + 1])
                vspace(2)
                v.tensor_single_scalar(C[:, 3:4], sb["gath"][:, 2 * j + 1:2 * j + 2], sb["sel_sb"][:, j:j + 1], ALU.mult)
                vspace(2)
                v.scalar_tensor_tensor(C[:, (j + 1) % 2:(j + 1) % 2 + 1], C[:, j % 2:j % 2 + 1],
                                       C[:, 2:3], C[:, 3:4], ALU.mult, ALU.add)
                vspace(2)
            carry = C[:, 4:5]
            v.tensor_copy(carry, C[:, 1:2])
            vspace(2)
            for c in range(NCH):
                cols = slice(c * CH, (c + 1) * CH)
                v.scalar_tensor_tensor(sb["h_loc"][:, cols], sb["cumP"][:, cols], carry,
                                       sb["h_loc"][:, cols], ALU.mult, ALU.add).then_inc(c_hzc, 1)
                QSS.wait_done(v, ss_z(c))
                v.tensor_mul(sb["hz_bf"][:, cols], sb["h_loc"][:, cols], sb[f"zb{c % 2}"][:, :]).then_inc(c_hzc, 1)
                if PH == "full":
                    if c == 3:
                        stts(0, 4)
                    elif c == 7:
                        stts(0, 5)
                    elif c == 11:
                        stts(0, 6)
                    elif c == 15:
                        stts(0, 7)
            if PH == "01S":
                return
            madds(0)
            for t in range(1, NT2):
                QOUT.wait_done(v, t - 1)
                for e in range(NE):
                    stts(t, e)
                madds(t)

        @block.scalar
        def _(s):
            s = _DrainingEngine(s)

            def ln_s(i):
                rows = 3 if i == 0 else 128
                if i == 0:
                    xg = sb["xbuf0"][0:3, 0:D]
                    QX.wait_done(s, 0)
                else:
                    t = gtile(i)
                    g = (i - 1) % G1
                    xb = sb["xbuf1"] if t % 2 == 0 else sb["xbuf0"]
                    xg = xb[:, g * D:(g + 1) * D]
                    QX.wait_done(s, t + 1)
                L = sb["lnbuf"]
                cst = sb["cst_sb"]
                o = (i % 2) * 8

                def Lc(c):
                    return L[0:rows, o + c:o + c + 1]

                s.activation(sb["sqdump"][0:rows, :], xg, AF.Identity, accum_out=Lc(0))
                s.activation(sb["sqdump"][0:rows, :], xg, AF.Square, accum_out=Lc(1))
                s.activation(Lc(2), Lc(0), AF.Identity, scale=cst[0:rows, 0:1])
                s.activation(Lc(3), Lc(1), AF.Identity, scale=cst[0:rows, 0:1],
                             bias=cst[0:rows, 1:2])
                s.activation(Lc(4), Lc(2), AF.Identity, scale=-1.0)
                s.activation(Lc(5), Lc(2), AF.Identity, scale=Lc(2))
                s.activation(Lc(6), Lc(5), AF.Identity, scale=-1.0, bias=Lc(3))
                s.activation(Lc(7), Lc(6), AF.Ln)
                s.activation(Lc(5), Lc(7), AF.Exp, scale=cst[0:rows, 2:3])
                if i >= 2:
                    s.wait_ge(c_tr, last_tr_of_group[i - 2])
                xnt = sb["xn_tm0"] if i % 2 == 0 else sb["xn_tm1"]
                s.activation(sb["sqdump"][0:rows, :], xg, AF.Identity, bias=Lc(4))
                s.activation(xnt[0:rows, :], sb["sqdump"][0:rows, :], AF.Identity,
                             scale=Lc(5)).then_inc(c_lns, 1)

            QWLD.wait_done(s, W_CST)
            ln_s(0)
            ln_s(1)
            for n, (kind, idx, k, ev) in enumerate(plan):
                if kind == "xn" and k == 0 and 1 <= idx and idx + 1 < NG:
                    ln_s(idx + 1)
                if ev[0] != "S":
                    continue
                s.wait_ge(c_tr, n + 1)
                if kind == "wi":
                    rows = 128 if idx < 16 else 16
                    s.activation(sb["wi_lhsT"][:, k, idx * 128:idx * 128 + rows],
                                 ptr[n % 2][0:128, 0:rows], AF.Copy).then_inc(c_evS, 1)
                elif kind == "wdbl":
                    s.activation(sb["wdbl_lhsT"][:, k, :], ptr[n % 2][0:128, 0:48],
                                 AF.Copy).then_inc(c_evS, 1)
                else:
                    s.activation(sb["wo16m"][:, k * 128:(k + 1) * 128], ptr[n % 2][0:16, 0:128],
                                 AF.Copy).then_inc(c_evS, 1)

            if PH in ("0", "Hm"):
                return
            QWLD.wait_done(s, W_BI)
            for m in range(16):
                s.wait_ge(c_mmxz, m + 1)
                if PH == "He":
                    s.activation(sb["tmp48"][:, 3 * m:3 * m + 3], pxz[m % 2][:, 0:3],
                                 AF.Identity).then_inc(c_evxz, 1)
                else:
                    s.activation(sb["tmp48"][:, 3 * m:3 * m + 3], pxz[m % 2][:, 0:3], AF.Identity,
                                 bias=sb["bi_sb"][:, m:m + 1]).then_inc(c_evxz, 1)
            if PH in ("He", "Heb"):
                return
            QWLD.wait_done(s, W_HM)
            s.activation(sb["haloxz"][:, :], sb["tmp48"][:, :], AF.Identity,
                         scale=sb["hm_sb"][:, 0:1])

            if PH == "H":
                return
            QWLD.wait_done(s, W_BC)
            QWLD.wait_done(s, W_B48)
            QWLD.wait_done(s, W_GB)
            for t in range(_NT1X):
                par = t % 2
                for m in range(17):
                    ps = pxz[(16 + 17 * t + m) % 2]
                    if m < 16:
                        u = 16 * t + m
                        cb = sb["cbuf0"] if m % 2 == 0 else sb["cbuf1"]
                        s.wait_ge(c_cv, max(0, u - 1))
                        s.activation(cb[:, 0:3], sb["haloxz"][:, 3 * m:3 * m + 3],
                                     AF.Identity).then_inc(c_hin, 1)
                        s.wait_ge(c_mmxz, evxz_cnt(t, m))
                        s.activation(cb[:, 3:515], ps[:, :], AF.Identity,
                                     bias=sb["bi_sb"][:, m:m + 1]).then_inc(c_evxz, 1)
                        s.activation(sb["haloxz"][:, 3 * m:3 * m + 3], cb[:, 512:515],
                                     AF.Identity)
                    else:
                        s.wait_ge(c_mmxz, evxz_cnt(t, m))
                        if t >= 2:
                            QST.wait_done(s, st_idx(t - 2, 1))
                        s.activation(sb[f"zstg{par}"][:, :], ps[0:16, :], AF.Identity,
                                     bias=sb["bi_sb"][0:16, 16:17]).then_inc(c_evxz, 1)
                    if m >= 1:
                        mm = m - 1
                        s.wait_ge(c_cv, 16 * t + mm + 1)
                        s.wait_ge(c_dblm, max(0, 16 * t + mm - 1))
                        pc = pcv[(16 * t + mm) % 2]
                        xc = sb["xcb0"] if mm % 2 == 0 else sb["xcb1"]
                        ins = s.activation(xc[:, :], pc[:, :], AF.Silu, bias=sb["bc_sb"][:, mm:mm + 1])
                        if mm == 0:
                            if t >= 2:
                                QST.wait_done(s, st_idx(t - 2, 0))
                            s.activation(sb[f"xstg{par}"][:, :], pc[0:16, :], AF.Silu,
                                         bias=sb["bc_sb"][0:16, 0:1]).then_inc(c_sil, 1)
                        else:
                            ins.then_inc(c_sil, 1)
                s.wait_ge(c_dblm, 16 * t + 16)
                if t >= 2:
                    QST.wait_done(s, st_idx(t - 2, 2))
                s.activation(sb[f"dstg{par}"][:, :], pdbl[:, :], AF.Identity,
                             bias=sb["b48_sb"][:, 0:1]).then_inc(c_evdbl, 1)
                s.wait_ge(c_gate, t + 1)
                if t >= 2:
                    QST.wait_done(s, st_idx(t - 2, 3))
                s.activation(sb[f"gstg{par}"][:, :], pgate[:, :], AF.Identity,
                             bias=sb["gb_sb"][:, 0:1]).then_inc(c_evgate, 1)

            if PH == "01" or _NT1X < NT1:
                return

            def chunk_acts(c):
                p = c % 2
                if c >= 2:
                    s.wait_ge(c_pp, 5 * (c - 2) + 5)
                QSS.wait_done(s, ss_chunk(c, 2))
                s.activation(sb[f"tS2{p}"][:, :], sb[f"inD{p}"][:, :], AF.Abs).then_inc(c_sp, 1)
                s.activation(sb[f"tS2{p}"][:, :], sb[f"tS2{p}"][:, :], AF.Exp, scale=-1.0).then_inc(c_sp, 1)
                s.activation(sb[f"tS2{p}"][:, :], sb[f"tS2{p}"][:, :], AF.Ln, bias=1.0).then_inc(c_sp, 1)
                QSS.wait_done(s, ss_chunk(c, 0))
                s.activation(sb[f"tS1{p}"][:, :], sb[f"inA{p}"][:, :], AF.Abs).then_inc(c_sp, 1)
                s.wait_ge(c_pp, 5 * c + 3)
                s.activation(sb[f"Abf{p}"][:, :], sb[f"tV1{p}"][:, :], AF.Exp, scale=-1.0).then_inc(c_sp, 1)

            QST.wait_all(s, 4 * NT1)
            s.activation(sb["sftn"][:, 0:1], sb["cst_sb"][:, 0:1], AF.Identity, scale=0.0)
            for gg in range(32):
                if gg >= 2:
                    s.wait_ge(c_gtb, gg - 1)
                    s.activation(sb["wfm"][0:8, 128 * (gg - 2):128 * (gg - 1)], pgb[:, :],
                                 AF.Copy).then_inc(c_wfm, 1)
                s.wait_ge(c_gtf, gg + 1)
                s.activation(sb["glog"][:, :], pgf[:, :], AF.Identity).then_inc(c_gse, 1)
                s.activation(sb["gmx"][:, :], sb["glog"][:, :], AF.Identity,
                             scale=sb["cst_sb"][:, 3:4], accum_out=sb["sftn"][:, 0:1])
                s.activation(sb["esb"][:, :], sb["glog"][:, :], AF.Exp,
                             bias=sb["sftn"][:, 0:1]).then_inc(c_gex, 1)
                if gg % 2 == 1:
                    chunk_acts(gg // 2)
            s.wait_ge(c_gtb, 31)
            s.activation(sb["wfm"][0:8, 128 * 30:128 * 31], pgb[:, :], AF.Copy).then_inc(c_wfm, 1)
            s.wait_ge(c_gtb, 32)
            s.activation(sb["wfm"][0:8, 128 * 31:128 * 32], pgb[:, :], AF.Copy).then_inc(c_wfm, 1)

    return nc


def kernel(L0_attn_x, L0_attn_w_in, L0_attn_b_in, L0_attn_w_conv, L0_attn_b_conv,
           L0_attn_w_xproj, L0_attn_w_dt, L0_attn_b_dt, L0_attn_w_out,
           L0_ffn_gate_w, L0_ffn_gate_b, L0_ffn_expert_w, L0_ffn_expert_b):
    x = np.ascontiguousarray(np.asarray(L0_attn_x, dtype=np.float32))
    w_in = np.asarray(L0_attn_w_in, dtype=np.float32)
    b_in = np.asarray(L0_attn_b_in, dtype=np.float32)
    w_conv = np.asarray(L0_attn_w_conv, dtype=np.float32)
    b_conv = np.ascontiguousarray(np.asarray(L0_attn_b_conv, dtype=np.float32))
    w_xproj = np.asarray(L0_attn_w_xproj, dtype=np.float32)
    w_dt = np.asarray(L0_attn_w_dt, dtype=np.float32)
    b_dt = np.asarray(L0_attn_b_dt, dtype=np.float32)
    w_out = np.asarray(L0_attn_w_out, dtype=np.float32)
    gate_w = np.ascontiguousarray(np.asarray(L0_ffn_gate_w, dtype=np.float32))
    gate_b = np.asarray(L0_ffn_gate_b, dtype=np.float32)
    expert_w = np.ascontiguousarray(np.asarray(L0_ffn_expert_w, dtype=np.float32))
    expert_b = np.ascontiguousarray(np.asarray(L0_ffn_expert_b, dtype=np.float32))

    if "nc" not in _CACHE:
        _CACHE["nc"] = _build()
    nc = _CACHE["nc"]

    wi = np.ascontiguousarray(w_in[0:2064, :])
    bi = np.zeros(2176, np.float32)
    bi[0:2064] = b_in[0:2064]
    wc = w_conv[:, 0, :]
    diag = np.zeros((16, 4, 128, 128), np.float32)
    ii = np.arange(128)
    for m in range(16):
        for tap in range(4):
            diag[m, tap, ii, ii] = wc[m * 128:(m + 1) * 128, tap]
    wdbl = np.ascontiguousarray(np.concatenate([w_xproj, w_dt], axis=0))
    b48 = np.ascontiguousarray(np.concatenate([np.zeros(32, np.float32), b_dt]).reshape(48, 1))
    wo16 = np.ascontiguousarray(w_out[:, 0:16])
    gb2 = np.ascontiguousarray(gate_b.reshape(8, 1))
    idin = np.eye(128, dtype=np.float32)
    cst = np.tile(np.array([[1.0 / D, EPS, -0.5, -0.125]], np.float32), (128, 1))

    x4 = x.reshape(2, 4, 4096, 1024)
    in_maps = []
    for core in range(8):
        seq, q = core // 4, core % 4
        xs = np.ascontiguousarray(x4[seq, q])
        if q == 0:
            xhs = np.zeros((3, 1024), np.float32)
            hm = np.zeros((128, 1), np.float32)
        else:
            xhs = np.ascontiguousarray(x[seq, q * 4096 - 3:q * 4096, :])
            hm = np.ones((128, 1), np.float32)
        sel = np.zeros((16, 3), np.float32)
        sel[:, 0:q] = 1.0
        selinv = np.ascontiguousarray((1.0 - sel).astype(np.float32))
        in_maps.append(dict(
            x=xs, xh=xhs, hmask=hm, cst=cst, sel=sel, selinv=selinv,
            wi=wi, bi=bi, wcdiag=diag, bc=b_conv, wdbl=wdbl, b48=b48,
            wo16=wo16, gw=gate_w, gb=gb2, ew=expert_w, eb=expert_b, idin=idin,
        ))
    res = run_bass_kernel_spmd(nc, in_maps, core_ids=list(range(8)))
    _CACHE["res"] = res
    out = np.empty((2, 16384, 1024), np.float32)
    for core in range(8):
        seq, q = core // 4, core % 4
        out[seq, q * 4096:(q + 1) * 4096, :] = res.results[core]["out"]
    return out
